# version 14
# baseline (speedup 1.0000x reference)
import numpy as np
import ml_dtypes
import concourse.bass as bass
import concourse.mybir as mybir
from concourse.bass_utils import run_bass_kernel_spmd

F32 = mybir.dt.float32
BF16 = mybir.dt.bfloat16
AF = mybir.ActivationFunctionType
ALU = mybir.AluOpType

B, S, D, H, HD = 8, 2048, 1024, 16, 64
NT = S // 128
NC = D // 128
LN2 = float(np.log(2.0))
SHIFT = -8.0

C_025, C_ILN2, C_LN2, C_EPS, C_SHIFT, C_ZERO = 0, 1, 2, 3, 4, 5


class Sched:

    DMA = {"SY", "G"}

    def __init__(self):
        self.items = []
        self.cnt = {"T": 0, "S": 0, "V": 0, "SY": 0, "G": 0, "GC": 0}
        self.batch_open = {"SY": False, "G": False}

    def add(self, eng, fn, waits=(), inc=True):
        waits = [w for w in waits if w is not None]
        if eng in self.DMA:
            self.items.append((eng, fn, waits, False))
            return None
        tok = None
        if inc:
            self.cnt[eng] += 1
            tok = (eng, self.cnt[eng])
        self.items.append((eng, fn, waits, inc))
        return tok

    def dma(self, eng, fn, waits=(), close=True):
        waits = [w for w in waits if w is not None]
        if not self.batch_open[eng]:
            if self.cnt[eng] > 0:
                waits = waits + [(eng, self.cnt[eng])]
            self.batch_open[eng] = True
        self.cnt[eng] += 16
        self.items.append((eng, fn, waits, True))
        if close:
            self.batch_open[eng] = False
            return (eng, self.cnt[eng])
        return None

    def barrier_tokens(self):
        return [(e, v) for e, v in self.cnt.items() if v > 0]

    def replay(self, eng_name, engine, sems):
        names = {"G": ("G", "GC")}.get(eng_name, (eng_name,))
        lastw = {}
        for eng, fn, waits, inc in self.items:
            if eng not in names:
                continue
            for we, wv in waits:
                if lastw.get(we, -1) >= wv:
                    continue
                engine.wait_ge(sems[we], wv)
                lastw[we] = wv
            inst = fn(engine)
            if inc:
                inst.then_inc(sems[eng], 16 if eng in self.DMA else 1)


def build_nc():
    nc = bass.Bass("TRN2", target_bir_lowering=False)

    def hin(name, shape, dt=F32):
        return nc.dram_tensor(name, list(shape), dt, kind="ExternalInput").ap()

    x_h = hin("x", (S, D))
    wq_h = hin("wqT", (D, D))
    wk_h = hin("wkT", (D, D))
    wv_h = hin("wvT", (D, D))
    wf_h = hin("wfT", (D, D))
    bq_h = hin("bq", (128, 8))
    bk_h = hin("bk", (128, 8))
    bv_h = hin("bvrep", (128, D))
    bf_h = hin("bfrep", (128, D))
    cos_h = hin("crep", (128, S))
    sin_h = hin("srep", (128, S))
    eye_h = hin("eye32", (128, 128))
    eyb_h = hin("eyebf", (128, 128), BF16)
    k_h = hin("consts", (128, 8))
    out_h = nc.dram_tensor("out", [S, D], F32, kind="ExternalOutput").ap()

    sched = Sched()
    A = sched.add

    live_cms = []

    def sb(name, shape, dt=F32, side="left"):
        cm = nc.sbuf_tensor(name, list(shape), dt, side=side)
        t = cm.__enter__()
        live_cms.append(cm)
        return t, cm

    def free(cm):
        cm.__exit__(None, None, None)
        live_cms.remove(cm)

    consts, _ = sb("k_s", (128, 8))
    eye32, _ = sb("eye32_s", (128, 128))
    eyebf, _ = sb("eyebf_s", (128, 128), BF16)
    bq_s, _ = sb("bq_s", (128, 8))
    bk_s, _ = sb("bk_s", (128, 8))
    cder, _ = sb("cder", (128, 4))
    rc_sb, _ = sb("rc_sb", (128, 16))
    stat, _ = sb("stat", (128, 24))

    qT, qT_cm = sb("qT", (128, NC, S), BF16)
    kT, kT_cm = sb("kT", (128, NC, S), BF16)
    va, va_cm = sb("va", (128, NT, H * 65), BF16)
    bv_s, bv_cm = sb("bv_s", (128, D), side="right")
    crep, crep_cm = sb("crep_s", (128, S), BF16, side="right")
    srep, srep_cm = sb("srep_s", (128, S), BF16, side="right")
    xT, xT_cm = sb("xT", (128, NC, S), BF16, side="right")
    wb, wb_cm = sb("wb", (128, 2, NC, D), BF16, side="right")
    xb16, xb16_cm = sb("xb16", (128, NT, D), BF16, side="right")

    psb0_cm = nc.psum_tensor("psb0", [128, 2048], BF16)
    psb0 = psb0_cm.__enter__()
    psb1_cm = nc.psum_tensor("psb1", [128, 2048], BF16)
    psb1 = psb1_cm.__enter__()

    DM = sched.dma
    DM("SY", lambda e: e.dma_start(out=consts[:], in_=k_h), close=False)
    DM("SY", lambda e: e.dma_start(out=eye32[:], in_=eye_h), close=False)
    DM("SY", lambda e: e.dma_start(out=eyebf[:], in_=eyb_h), close=False)
    DM("SY", lambda e: e.dma_start(out=bq_s[:], in_=bq_h), close=False)
    DM("SY", lambda e: e.dma_start(out=bk_s[:], in_=bk_h), close=False)
    t_P0 = DM("SY", lambda e: e.dma_start(out=bv_s[:], in_=bv_h), close=True)
    t_k = t_eye = t_eyb = t_bq = t_bk = t_bv = t_P0
    x3 = x_h.rearrange("(st p) d -> p st d", p=128)
    t_xb = DM("G", lambda e: e.dma_start(out=xb16[:], in_=x3), close=False)
    DM("G", lambda e: e.dma_start(out=crep[:], in_=cos_h), close=False)
    t_tab = DM("G", lambda e: e.dma_start(out=srep[:], in_=sin_h), close=True)
    t_xb = t_cos = t_sin = t_tab
    wq3 = wq_h.rearrange("(ct p) d -> p ct d", p=128)
    t_wq = DM("G", lambda e: e.dma_start(out=wb[:, 0], in_=wq3))
    wk3 = wk_h.rearrange("(ct p) d -> p ct d", p=128)
    t_wk = DM("G", lambda e: e.dma_start(out=wb[:, 1], in_=wk3))
    wv3 = wv_h.rearrange("(ct p) d -> p ct d", p=128)

    t_p0 = A("V", lambda e: e.memset(cder[:, 2:3], 1.0))
    t_p1 = A("V", lambda e: e.tensor_scalar_add(cder[:, 3:4], cder[:, 2:3], 2.0 ** -30),
             waits=[t_p0])
    t_p2 = A("V", lambda e: e.tensor_scalar(cder[:, 2:3], cder[:, 3:4], 1.0, 2.0 ** 30,
                                            ALU.subtract, ALU.mult), waits=[t_p1])
    t_C0 = A("V", lambda e: e.tensor_scalar(cder[:, 0:1], cder[:, 2:3],
                                            1.5 * 2.0 ** 52 - 1.5 * 2.0 ** 23,
                                            1.5 * 2.0 ** 23, ALU.mult, ALU.add),
             waits=[t_p2])
    t_C67 = A("V", lambda e: e.tensor_scalar_add(cder[:, 1:2], cder[:, 0:1], 67.0),
              waits=[t_C0])
    t_ones = A("V", lambda e: e.memset(va[:], 1.0))

    tr_evac = {0: None, 1: None}
    last_tr = None
    for ct in range(NC):
        par = ct % 2
        ps = psb0 if par == 0 else psb1
        deps = [t_xb, t_eyb]
        if ct >= 2:
            deps.append(tr_evac[par])
        for st in range(NT):
            def trf(e, st=st, ct=ct, ps=ps):
                return e.transpose(ps[:, st * 128:(st + 1) * 128],
                                   xb16[:, st, ct * 128:(ct + 1) * 128], eyebf[:])
            last_tr = A("T", trf, waits=deps if st == 0 else [], inc=(st == NT - 1))
        def evf(e, ct=ct, ps=ps):
            return e.tensor_copy(xT[:, ct, :], ps[:])
        tr_evac[par] = A("V", evf, waits=[last_tr])
    t_xT = tr_evac[(NC - 1) % 2]
    psb1_cm.__exit__(None, None, None)
    psb0_cm.__exit__(None, None, None)
    ps1_cm = nc.psum_tensor("ps1", [128, 2048], F32)
    ps1 = ps1_cm.__enter__()
    ps0_cm = nc.psum_tensor("ps0", [128, 2048], F32)
    ps0 = ps0_cm.__enter__()

    free(xb16_cm)
    qraw, qraw_cm = sb("qraw", (128, 2, S), BF16, side="right")
    rs, rs_cm = sb("rs", (128, 4, 1024), BF16, side="right")
    ro, ro_cm = sb("ro", (128, 2, S), BF16, side="right")

    kq_weight_tok = {0: t_wq, 1: t_wk}
    proj_srcs = [(0, bq_s, qT), (1, bk_s, kT)]
    band_toks = []
    psum_free = {0: tr_evac[0], 1: tr_evac[1]}
    prev_rope_read = None
    prev_rope_read_g = None
    prev_band = {0: None, 1: None}
    w_last_use = None
    t_wv = None

    for pi, (pidx, bias_s, dstT) in enumerate(proj_srcs):
        t_w = kq_weight_tok[pidx]
        for g in range(4):
            pair_evacs = []
            for slot, dt in enumerate((g, g + 4)):
                idx = pi * 8 + g * 2 + slot
                par = idx % 2
                ps = ps0 if par == 0 else ps1
                mm_waits = [t_w, t_xT, psum_free.get(par)]
                last = None
                for cc in range(NC):
                    for nq in range(4):
                        def mmf(e, cc=cc, nq=nq, dt=dt, ps=ps, pi=pi):
                            return e.matmul(ps[:, nq * 512:(nq + 1) * 512],
                                            wb[:, pi, cc, dt * 128:(dt + 1) * 128],
                                            xT[:, cc, nq * 512:(nq + 1) * 512],
                                            start=(cc == 0), stop=(cc == NC - 1))
                        last = A("T", mmf,
                                 waits=mm_waits if (cc == 0 and nq == 0) else [],
                                 inc=(cc == NC - 1 and nq == 3))
                w_last_use = last
                ev_waits = [last, t_bq if pidx == 0 else t_bk, prev_rope_read,
                            prev_rope_read_g]
                def evf(e, slot=slot, ps=ps, bias_s=bias_s, dt=dt):
                    return e.tensor_scalar_add(qraw[:, slot, :], ps[:],
                                               bias_s[:, dt:dt + 1])
                tok = A("V", evf, waits=ev_waits)
                psum_free[par] = tok
                pair_evacs.append(tok)
            rope_deps = pair_evacs + [t_cos, t_sin,
                                      prev_band[pi] or prev_band[1 - pi]]
            v_done = None
            g_done = None
            for hh in range(2):
                sl = slice(hh * 1024, (hh + 1) * 1024)
                t1 = A("V", lambda e, sl=sl: e.tensor_mul(rs[:, 0, :], qraw[:, 0, sl], crep[:, sl]),
                       waits=rope_deps if hh == 0 else [v_done])
                t2 = A("V", lambda e, sl=sl: e.tensor_mul(rs[:, 1, :], qraw[:, 1, sl], srep[:, sl]))
                v_done = A("V", lambda e, sl=sl: e.tensor_sub(ro[:, 0, sl], rs[:, 0, :], rs[:, 1, :]),
                           waits=[t1, t2])
                g1 = A("GC", lambda e, sl=sl: e.tensor_mul(rs[:, 2, :], qraw[:, 0, sl], srep[:, sl]),
                       waits=rope_deps if hh == 0 else [g_done])
                g2 = A("GC", lambda e, sl=sl: e.tensor_mul(rs[:, 3, :], qraw[:, 1, sl], crep[:, sl]))
                g_done = A("GC", lambda e, sl=sl: e.tensor_add(ro[:, 1, sl], rs[:, 2, :], rs[:, 3, :]),
                           waits=[g1, g2])
            last_ro = v_done
            last_ro_g = g_done
            prev_rope_read = v_done
            prev_rope_read_g = g_done
            bt = None
            for a in range(4):
                h = 4 * g + a
                tt_ = h // 2
                base = (h % 2) * 64
                def bdr(e, a=a, tt_=tt_, base=base, dstT=dstT):
                    return e.dma_start(out=dstT[base:base + 32, tt_, :],
                                       in_=ro[a * 32:(a + 1) * 32, 0, :])
                def bdi(e, a=a, tt_=tt_, base=base, dstT=dstT):
                    return e.dma_start(out=dstT[base + 32:base + 64, tt_, :],
                                       in_=ro[a * 32:(a + 1) * 32, 1, :])
                DM("SY", bdr, waits=[last_ro, last_ro_g] if a == 0 else [], close=False)
                bt = DM("SY", bdi, close=(a == 3))
            prev_band[pi] = bt
            band_toks.append(bt)
        if pi == 0:
            t_wv = DM("G", lambda e: e.dma_start(out=wb[:, 0], in_=wv3),
                      waits=[w_last_use])

    last_v_evac = None
    for st in range(NT):
        par = st % 2
        ps = ps0 if par == 0 else ps1
        mm_waits = [t_wv, t_xT, psum_free.get(par)]
        last = None
        for cc in range(NC):
            for nv in range(2):
                def mmv(e, cc=cc, nv=nv, st=st, ps=ps):
                    return e.matmul(ps[:, nv * 512:(nv + 1) * 512],
                                    xT[:, cc, st * 128:(st + 1) * 128],
                                    wb[:, 0, cc, nv * 512:(nv + 1) * 512],
                                    start=(cc == 0), stop=(cc == NC - 1))
                last = A("T", mmv, waits=mm_waits if (cc == 0 and nv == 0) else [],
                         inc=(cc == NC - 1 and nv == 1))
        def evv(e, st=st, ps=ps):
            pa = ps.ap()[:, 0:1024].rearrange("p (h d) -> p h d", d=64)
            bva = bv_s.ap().rearrange("p (h d) -> p h d", d=64)
            dst = va.ap()[:, st, :].rearrange("p (h d) -> p h d", d=65)[:, :, 0:64]
            return e.tensor_add(dst, pa, bva)
        tok = A("V", evv, waits=[last, t_bv, t_ones])
        psum_free[par] = tok
        last_v_evac = tok

    free(ro_cm); free(rs_cm); free(qraw_cm); free(wb_cm)
    free(xT_cm); free(srep_cm); free(crep_cm); free(bv_cm)
    xf, xf_cm = sb("xf", (128, NT, D), side="right")
    wfs, wfs_cm = sb("wfs", (128, 1024), side="right")
    wfq, wfq_cm = sb("wfq", (128, NC, D), BF16, side="right")
    qs_t, qs_cm = sb("qsc", (128, 3, 1024), side="right")
    PT, PT_cm = sb("PT", (128, 2, 1024), BF16, side="right")
    oa_sb, oa_cm = sb("oa_sb", (128, 1024), side="right")

    ps0_cm.__exit__(None, None, None)
    ps_sc_cm = nc.psum_tensor("ps_sc", [128, 2, 1024], F32)
    ps_sc = ps_sc_cm.__enter__()

    bar2 = sched.barrier_tokens()

    t_xf = DM("SY", lambda e: e.dma_start(out=xf[:], in_=x3), waits=bar2)

    def quant_chain(src_fn, dst_fn, deps, extra_dst_waits=(), qs=None):
        if qs is None:
            qs = qs_t
        tA = A("S", lambda e: e.activation(qs[:, 0, :], src_fn(), AF.Abs,
                                           bias=0.0, scale=512.0), waits=list(deps))
        tL = A("S", lambda e: e.activation(qs[:, 1, :], qs[:, 0, :], AF.Ln,
                                           bias=consts[:, C_025:C_025 + 1], scale=1.0),
               waits=[tA, t_k])
        tM = A("V", lambda e: e.tensor_scalar(qs[:, 0, :], qs[:, 1, :],
                                              consts[:, C_ILN2:C_ILN2 + 1], 63.5,
                                              ALU.mult, ALU.add), waits=[tL, t_k])
        tE = A("V", lambda e: e.tensor_scalar(qs[:, 1, :], qs[:, 0, :],
                                              cder[:, 0:1], cder[:, 1:2],
                                              ALU.add, ALU.subtract), waits=[tM, t_C67])
        tS = A("S", lambda e: e.activation(qs[:, 0, :], qs[:, 1, :], AF.Exp,
                                           bias=0.0, scale=consts[:, C_LN2:C_LN2 + 1]),
               waits=[tE])
        tMt = A("V", lambda e: e.tensor_scalar(qs[:, 1, :], qs[:, 0, :], 1.0,
                                               cder[:, 0:1], ALU.max, ALU.mult),
                waits=[tS])
        tP = A("V", lambda e: e.scalar_tensor_tensor(qs[:, 0, :], src_fn(), 512.0,
                                                     qs[:, 1, :], ALU.mult, ALU.add),
               waits=[tMt] + list(deps))
        tQ = A("V", lambda e: e.tensor_sub(qs[:, 2, :], qs[:, 0, :], qs[:, 1, :]),
               waits=[tP])
        tZ = A("V", lambda e: e.tensor_scalar_mul(dst_fn(), qs[:, 2, :], 2.0 ** -9),
               waits=[tQ] + list(extra_dst_waits))
        return tZ

    wf3 = wf_h.rearrange("(ct p) d -> p ct d", p=128)
    wf_toks = []

    wf_steps = []

    def build_wf_steps():
        state = {}
        for cc in range(NC):
            def s_ld(cc=cc):
                ld_waits = list(bar2) if cc == 0 else [wf_toks[cc - 1]]
                state["ld"] = DM("G", lambda e, cc=cc: e.dma_start(
                    out=wfs[:], in_=wf3[:, cc, :]), waits=ld_waits)
            def s_A(cc=cc):
                state["A"] = A("S", lambda e: e.activation(
                    qs_t[:, 0, :], wfs[:], AF.Abs, bias=0.0, scale=512.0),
                    waits=[state["ld"]] + ([wf_toks[cc - 1]] if cc else []))
            def s_L(cc=cc):
                state["L"] = A("S", lambda e: e.activation(
                    qs_t[:, 1, :], qs_t[:, 0, :], AF.Ln,
                    bias=consts[:, C_025:C_025 + 1], scale=1.0),
                    waits=[state["A"], t_k])
            def s_M(cc=cc):
                state["M"] = A("V", lambda e: e.tensor_scalar(
                    qs_t[:, 0, :], qs_t[:, 1, :], consts[:, C_ILN2:C_ILN2 + 1],
                    63.5, ALU.mult, ALU.add), waits=[state["L"], t_k])
            def s_E(cc=cc):
                state["E"] = A("V", lambda e: e.tensor_scalar(
                    qs_t[:, 1, :], qs_t[:, 0, :], cder[:, 0:1], cder[:, 1:2],
                    ALU.add, ALU.subtract), waits=[state["M"], t_C67])
            def s_S(cc=cc):
                state["S"] = A("S", lambda e: e.activation(
                    qs_t[:, 0, :], qs_t[:, 1, :], AF.Exp, bias=0.0,
                    scale=consts[:, C_LN2:C_LN2 + 1]), waits=[state["E"]])
            def s_Mt(cc=cc):
                state["Mt"] = A("V", lambda e: e.tensor_scalar(
                    qs_t[:, 1, :], qs_t[:, 0, :], 1.0, cder[:, 0:1],
                    ALU.max, ALU.mult), waits=[state["S"]])
            def s_P(cc=cc):
                state["P"] = A("V", lambda e: e.scalar_tensor_tensor(
                    qs_t[:, 0, :], wfs[:], 512.0, qs_t[:, 1, :],
                    ALU.mult, ALU.add), waits=[state["Mt"]])
            def s_Q(cc=cc):
                state["Q"] = A("V", lambda e: e.tensor_sub(
                    qs_t[:, 2, :], qs_t[:, 0, :], qs_t[:, 1, :]), waits=[state["P"]])
            def s_Z(cc=cc):
                tz = A("V", lambda e, cc=cc: e.tensor_scalar_mul(
                    wfq[:, cc, :], qs_t[:, 2, :], 2.0 ** -9), waits=[state["Q"]])
                wf_toks.append(tz)
            wf_steps.extend([s_ld, s_A, s_L, s_M, s_E, s_S, s_Mt, s_P, s_Q, s_Z])

    build_wf_steps()

    qk_ready = band_toks[-1]
    va_ready = last_v_evac
    prev_div = None
    prev_ev = None
    prev_exp = None
    prev_pv = {}
    first_attn = [qk_ready, va_ready, psum_free[0], psum_free[1]] + bar2
    tile_map = [(t // 4) * 512 + (t % 4) * 65 for t in range(8)]
    attn_done = None

    exp_hist = {}
    prev_trp_ref = [None]
    prev_div_ref = [None]

    def attention_head(qc, hp, first):
        hA, hB = 2 * hp, 2 * hp + 1
        q0 = qc * 512

        def emit_scores(kt, carry):
            slot = kt % 2
            sc_waits = []
            if first and kt == 0:
                sc_waits += list(first_attn)
            sc_waits.append(exp_hist.get(kt - 2, carry))
            last_sc = None
            for half in range(2):
                r0 = half * 64
                def scf(e, half=half, r0=r0, kt=kt, hp=hp, q0=q0, slot=slot):
                    return e.matmul(
                        ps_sc[:, slot, half * 512:(half + 1) * 512],
                        kT[r0:r0 + 64, hp, kt * 128:(kt + 1) * 128],
                        qT[r0:r0 + 64, hp, q0:q0 + 512],
                        start=True, stop=True, tile_position=(r0, 0))
                last_sc = A("T", scf, waits=sc_waits if half == 0 else [],
                            inc=(half == 1))
            return last_sc

        def emit_exp(kt, last_sc):
            slot = kt % 2
            ex_waits = [last_sc, t_k, prev_pv.get(slot)]
            def exf(e, slot=slot):
                return e.activation(PT[:, slot, :], ps_sc[:, slot, :], AF.Exp,
                                    bias=consts[:, C_SHIFT:C_SHIFT + 1], scale=0.125)
            t_ex = A("S", exf, waits=ex_waits)
            exp_hist[kt] = t_ex
            return t_ex

        def emit_pv(kt, t_ex, pset):
            slot = kt % 2
            pv_waits = [t_ex]
            if kt == 0:
                pv_waits += [va_ready, div_done[pset]] + (list(bar2) if first else [])
            last_pv = None
            for half, h in ((0, hA), (1, hB)):
                def pvf(e, half=half, h=h, kt=kt, slot=slot, pset=pset):
                    base = pset * 1024 + half * 512
                    return e.matmul(
                        ps1[0:65, base:base + 512],
                        va[:, kt, h * 65: h * 65 + 65],
                        PT[:, slot, half * 512:(half + 1) * 512],
                        start=(kt == 0), stop=(kt == NT - 1))
                last_pv = A("T", pvf, waits=pv_waits if half == 0 else [],
                            inc=(half == 1))
            prev_pv[slot] = last_pv
            return last_pv

        return emit_scores, emit_exp, emit_pv, (hA, hB)

    def post_evac(hA, hB, qc, last_pv, pset):
        base = pset * 1024
        def eva(e, base=base):
            return e.tensor_copy(oa_sb[0:65, :], ps1[0:65, base:base + 1024])
        t_ev = A("V", eva, waits=[last_pv, trp_done[0], trp_done[1]])
        return (hA, hB, qc, pset, t_ev)

    def post_transpose(state, j, last):
        hA, hB, qc, pset, t_ev = state
        base = pset * 1024
        half, t = j // 4, j % 4
        col0 = base + half * 512 + t * 65
        def trf2(e, half=half, t=t, col0=col0):
            return e.transpose(ps1[:, col0:col0 + 65],
                               oa_sb[0:65, half * 512 + t * 128: half * 512 + (t + 1) * 128],
                               eye32[0:65, 0:65])
        return A("T", trf2, waits=[t_ev, t_eye] if j == 0 else [], inc=(j == 7))

    def post_finish(state, last_trp):
        hA, hB, qc, pset, t_ev = state
        base = pset * 1024
        trp_done[pset] = last_trp
        last_rc = None
        for j in range(8):
            half, t = j // 4, j % 4
            col0 = base + half * 512 + t * 65
            def rcf(e, j=j, col0=col0):
                return e.reciprocal(rc_sb[:, j:j + 1], ps1[:, col0 + 64:col0 + 65])
            last_rc = A("V", rcf,
                        waits=[last_trp, prev_div_ref[0]] if j == 0 else [],
                        inc=(j == 7))
        last_div = None
        for j in range(8):
            half, t = j // 4, j % 4
            h = hA if half == 0 else hB
            col0 = base + half * 512 + t * 65
            qtile = qc * 4 + t
            def dvf(e, h=h, j=j, col0=col0, qtile=qtile):
                xsl = xf[:, qtile, h * 64:(h + 1) * 64]
                return e.scalar_tensor_tensor(
                    xsl, ps1[:, col0:col0 + 64], rc_sb[:, j:j + 1],
                    xsl, ALU.mult, ALU.add)
            last_div = A("V", dvf, waits=[last_rc, t_xf] if j == 0 else [],
                         inc=(j == 7))
        div_done[pset] = last_div
        prev_div_ref[0] = last_div

    def emit_post(hA, hB, qc, last_pv, pset):
        state = post_evac(hA, hB, qc, last_pv, pset)
        last = None
        for j in range(8):
            last = post_transpose(state, j, last)
        post_finish(state, last)

    heads = [(qc, hp) for qc in range(4) for hp in range(8)]
    post_q = []
    div_done = {0: None, 1: None}
    trp_done = {0: None, 1: None}
    pending_pv = None
    for idx, (qc, hp) in enumerate(heads):
        emit_scores, emit_exp, emit_pv, (hA, hB) = attention_head(qc, hp, idx == 0)
        pset = idx % 2
        carry = prev_exp
        sc_tok = emit_scores(0, carry)
        if pending_pv is not None:
            p_emit, e14, e15, p_set, p_args = pending_pv
            p_emit(NT - 2, e14, p_set)
        emit_exp(0, sc_tok)
        sc_tok = emit_scores(1, carry)
        if pending_pv is not None:
            last_pv_prev = p_emit(NT - 1, e15, p_set)
            post_q.append(p_args + (last_pv_prev, p_set))
            pending_pv = None
        emit_exp(1, sc_tok)
        post_state = [None, None]
        for kt in range(2, NT):
            sc_tok = emit_scores(kt, carry)
            emit_pv(kt - 2, exp_hist[kt - 2], pset)
            emit_exp(kt, sc_tok)
            if kt in (4, 9, 13) and wf_steps and idx >= 4:
                wf_steps.pop(0)()
            if kt == 4 and post_q:
                post_state[0] = post_evac(*post_q.pop(0))
            if post_state[0] is not None and 5 <= kt <= 12:
                post_state[1] = post_transpose(post_state[0], kt - 5, post_state[1])
            if kt == 13 and post_state[0] is not None:
                post_finish(post_state[0], post_state[1])
                post_state = [None, None]
        pending_pv = (emit_pv, exp_hist[NT - 2], exp_hist[NT - 1], pset,
                      (hA, hB, qc))
        prev_exp = exp_hist[NT - 1]
        exp_hist = {}
    p_emit, e14, e15, p_set, p_args = pending_pv
    p_emit(NT - 2, e14, p_set)
    last_pv_prev = p_emit(NT - 1, e15, p_set)
    post_q.append(p_args + (last_pv_prev, p_set))
    while post_q:
        emit_post(*post_q.pop(0))
    attn_done = prev_div_ref[0]
    attn_done = prev_div_ref[0]

    free(oa_cm); free(PT_cm)
    free(va_cm); free(kT_cm); free(qT_cm)
    ps_sc_cm.__exit__(None, None, None)
    psb4_cm = nc.psum_tensor("psb4", [128, 2, 1024], BF16)
    psb4 = psb4_cm.__enter__()
    yqT, yqT_cm = sb("yqT", (128, NC, S), BF16)
    yqp, yqp_cm = sb("yqp", (128, 4, 1024), BF16)
    zf, zf_cm = sb("zf", (128, 4, 1024))
    bf_s, bf_cm = sb("bf_s", (128, D))
    qs2, qs2_cm = sb("qs2", (128, 3, 1024))
    dump_t, _dump_cm = sb("dump_t", (128, 1024))
    yf, yf_cm = sb("yf", (128, 5, 1024))
    bar3 = sched.barrier_tokens()
    t_bf = DM("SY", lambda e: e.dma_start(out=bf_s[:], in_=bf_h), waits=bar3)

    def layernorm(src_fn, dst_fn, deps, c0, dst_waits=()):
        tsum = A("S", lambda e: e.activation(wfs[:], src_fn(), AF.Copy,
                                             bias=0.0, scale=1.0,
                                             accum_out=stat[:, c0:c0 + 1]),
                 waits=list(deps) + [t_wfq])
        tsq = A("S", lambda e: e.activation(wfs[:], src_fn(), AF.Square,
                                            bias=0.0, scale=1.0,
                                            accum_out=stat[:, c0 + 1:c0 + 2]),
                waits=[tsum])
        tmean = A("V", lambda e: e.tensor_scalar_mul(stat[:, c0 + 2:c0 + 3],
                                                     stat[:, c0:c0 + 1], 1.0 / D),
                  waits=[tsum])
        tmsq = A("V", lambda e: e.tensor_mul(stat[:, c0 + 3:c0 + 4],
                                             stat[:, c0 + 2:c0 + 3],
                                             stat[:, c0 + 2:c0 + 3]), waits=[tmean])
        tvar = A("V", lambda e: e.scalar_tensor_tensor(
            stat[:, c0:c0 + 1], stat[:, c0 + 1:c0 + 2], 1.0 / D,
            stat[:, c0 + 3:c0 + 4], ALU.mult, ALU.subtract), waits=[tsq, tmsq])
        tln = A("S", lambda e: e.activation(stat[:, c0 + 1:c0 + 2], stat[:, c0:c0 + 1],
                                            AF.Ln, bias=consts[:, C_EPS:C_EPS + 1],
                                            scale=1.0), waits=[tvar, t_k])
        trstd = A("S", lambda e: e.activation(stat[:, c0:c0 + 1], stat[:, c0 + 1:c0 + 2],
                                              AF.Exp, bias=0.0, scale=-0.5), waits=[tln])
        ts = A("V", lambda e: e.tensor_scalar(dst_fn(), src_fn(),
                                              stat[:, c0 + 2:c0 + 3],
                                              stat[:, c0:c0 + 1],
                                              ALU.subtract, ALU.mult),
               waits=[trstd] + list(dst_waits))
        return ts

    st_tok = [dict() for _ in range(NT)]
    out_toks = []
    rot_use = {"yf": {}, "yqp": {}, "zf": {}, "qs": {}, "psb4": {}, "ffn": {},
               "st1": {}, "st2": {}, "dump": [None]}

    def stage_fn(stg, st):
        tk = st_tok[st]
        par = st % 2
        c0 = (st % 4) * 4
        c2_ = 16 + par * 4
        if stg == 0:
            deps = [attn_done, rot_use["st1"].get(st % 4)]
            tk["sum"] = A("V", lambda e: e.reduce_sum(stat[:, c0:c0 + 1], xf[:, st, :],
                                                      axis=mybir.AxisListType.X),
                          waits=deps)
            tk["sq"] = A("S", lambda e: e.activation(dump_t[:], xf[:, st, :], AF.Square,
                                                     bias=0.0, scale=1.0,
                                                     accum_out=stat[:, c0 + 1:c0 + 2]),
                         waits=deps + [rot_use["dump"][0]])
            rot_use["dump"][0] = tk["sq"]
        elif stg == 1:
            tk["mean"] = A("V", lambda e: e.tensor_scalar_mul(stat[:, c0 + 2:c0 + 3],
                                                              stat[:, c0:c0 + 1], 1.0 / D),
                           waits=[tk["sum"]])
            tk["msq"] = A("V", lambda e: e.tensor_mul(stat[:, c0 + 3:c0 + 4],
                                                      stat[:, c0 + 2:c0 + 3],
                                                      stat[:, c0 + 2:c0 + 3]),
                          waits=[tk["mean"]])
            tk["var"] = A("V", lambda e: e.scalar_tensor_tensor(
                stat[:, c0:c0 + 1], stat[:, c0 + 1:c0 + 2], 1.0 / D,
                stat[:, c0 + 3:c0 + 4], ALU.mult, ALU.subtract),
                waits=[tk["sq"], tk["msq"]])
            tk["ln"] = A("S", lambda e: e.activation(stat[:, c0 + 1:c0 + 2],
                                                     stat[:, c0:c0 + 1], AF.Ln,
                                                     bias=consts[:, C_EPS:C_EPS + 1],
                                                     scale=1.0), waits=[tk["var"], t_k])
            tk["rstd"] = A("S", lambda e: e.activation(stat[:, c0:c0 + 1],
                                                       stat[:, c0 + 1:c0 + 2], AF.Exp,
                                                       bias=0.0, scale=-0.5),
                           waits=[tk["ln"]])
        elif stg == 2:
            ys = st % 5
            tk["y"] = A("V", lambda e: e.tensor_scalar(yf[:, ys, :], xf[:, st, :],
                                                       stat[:, c0 + 2:c0 + 3],
                                                       stat[:, c0:c0 + 1],
                                                       ALU.subtract, ALU.mult),
                        waits=[tk["rstd"], rot_use["yf"].get(ys)])
            rot_use["st1"][st % 4] = tk["y"]
            qs = qs_t if par == 0 else qs2
            tk["qA"] = A("S", lambda e: e.activation(qs[:, 0, :], yf[:, ys, :], AF.Abs,
                                                     bias=0.0, scale=512.0),
                         waits=[tk["y"], rot_use["qs"].get(par)])
            tk["qL"] = A("S", lambda e: e.activation(qs[:, 1, :], qs[:, 0, :], AF.Ln,
                                                     bias=consts[:, C_025:C_025 + 1],
                                                     scale=1.0), waits=[tk["qA"]])
        elif stg == 3:
            qs = qs_t if par == 0 else qs2
            tk["qM"] = A("V", lambda e: e.tensor_scalar(qs[:, 0, :], qs[:, 1, :],
                                                        consts[:, C_ILN2:C_ILN2 + 1],
                                                        63.5, ALU.mult, ALU.add),
                         waits=[tk["qL"], t_k])
            tk["qE"] = A("V", lambda e: e.tensor_scalar(qs[:, 1, :], qs[:, 0, :],
                                                        cder[:, 0:1], cder[:, 1:2],
                                                        ALU.add, ALU.subtract),
                         waits=[tk["qM"], t_C67])
            tk["qS"] = A("S", lambda e: e.activation(qs[:, 0, :], qs[:, 1, :], AF.Exp,
                                                     bias=0.0,
                                                     scale=consts[:, C_LN2:C_LN2 + 1]),
                         waits=[tk["qE"]])
        elif stg == 4:
            qs = qs_t if par == 0 else qs2
            ys = st % 5
            tk["qMt"] = A("V", lambda e: e.tensor_scalar(qs[:, 1, :], qs[:, 0, :], 1.0,
                                                         cder[:, 0:1], ALU.max, ALU.mult),
                          waits=[tk["qS"]])
            tk["qP"] = A("V", lambda e: e.scalar_tensor_tensor(qs[:, 0, :], yf[:, ys, :],
                                                               512.0, qs[:, 1, :],
                                                               ALU.mult, ALU.add),
                         waits=[tk["qMt"]])
            tk["qQ"] = A("V", lambda e: e.tensor_sub(qs[:, 2, :], qs[:, 0, :],
                                                     qs[:, 1, :]), waits=[tk["qP"]])
            qp = st % 4
            tk["qZ"] = A("V", lambda e: e.tensor_scalar_mul(yqp[:, qp, :], qs[:, 2, :],
                                                            2.0 ** -9),
                         waits=[tk["qQ"], rot_use["yqp"].get(qp)])
            rot_use["qs"][par] = tk["qZ"]
        elif stg == 5:
            qp = st % 4
            last_tq = None
            tr_waits = [tk["qZ"], rot_use["psb4"].get(par)]
            for ctt in range(NC):
                def tqf(e, ctt=ctt, par=par, qp=qp):
                    return e.transpose(psb4[:, par, ctt * 128:(ctt + 1) * 128],
                                       yqp[:, qp, ctt * 128:(ctt + 1) * 128], eyebf[:])
                last_tq = A("T", tqf, waits=tr_waits if ctt == 0 else [],
                            inc=(ctt == NC - 1))
            rot_use["yqp"][qp] = last_tq
            def tqe(e, st=st, par=par):
                srcp = psb4.ap()[:, par, :].rearrange("p (ct c) -> p ct c", c=128)
                dst = yqT.ap()[:, :, st * 128:(st + 1) * 128]
                return e.activation(dst, srcp, AF.Copy, bias=0.0, scale=1.0)
            tk["tqe"] = A("S", tqe, waits=[last_tq])
            rot_use["psb4"][par] = tk["tqe"]
        elif stg == 6:
            poff = par * 1024
            mm_waits = [tk["tqe"], rot_use["ffn"].get(par)]
            last = None
            for cc in range(NC):
                mm_waits_cc = [wf_toks[cc]] + (mm_waits if cc == 0 else [])
                for c2 in range(2):
                    def mf(e, cc=cc, c2=c2, st=st, poff=poff):
                        return e.matmul(ps1[:, poff + c2 * 512: poff + (c2 + 1) * 512],
                                        yqT[:, cc, st * 128:(st + 1) * 128],
                                        wfq[:, cc, c2 * 512:(c2 + 1) * 512],
                                        start=(cc == 0), stop=(cc == NC - 1))
                    last = A("T", mf, waits=mm_waits_cc if c2 == 0 else [],
                             inc=(cc == NC - 1 and c2 == 1))
            zs = st % 4
            tk["z1"] = A("V", lambda e, zs=zs, poff=poff: e.tensor_add(
                zf[:, zs, :], ps1[:, poff:poff + 1024], bf_s[:]),
                waits=[last, t_bf, rot_use["zf"].get(zs)])
            rot_use["ffn"][par] = tk["z1"]
            ys = st % 5
            tk["z2"] = A("V", lambda e, zs=zs, ys=ys: e.tensor_add(
                zf[:, zs, :], zf[:, zs, :], yf[:, ys, :]), waits=[tk["z1"]])
            rot_use["yf"][ys] = tk["z2"]
        elif stg == 7:
            zs = st % 4
            deps = [tk["z2"], rot_use["st2"].get(par)]
            tk["sum2"] = A("V", lambda e, zs=zs: e.reduce_sum(
                stat[:, c2_:c2_ + 1], zf[:, zs, :], axis=mybir.AxisListType.X),
                waits=deps)
            tk["sq2"] = A("S", lambda e, zs=zs: e.activation(
                dump_t[:], zf[:, zs, :], AF.Square, bias=0.0, scale=1.0,
                accum_out=stat[:, c2_ + 1:c2_ + 2]),
                waits=deps + [rot_use["dump"][0]])
            rot_use["dump"][0] = tk["sq2"]
            tk["mean2"] = A("V", lambda e: e.tensor_scalar_mul(
                stat[:, c2_ + 2:c2_ + 3], stat[:, c2_:c2_ + 1], 1.0 / D),
                waits=[tk["sum2"]])
            tk["msq2"] = A("V", lambda e: e.tensor_mul(
                stat[:, c2_ + 3:c2_ + 4], stat[:, c2_ + 2:c2_ + 3],
                stat[:, c2_ + 2:c2_ + 3]), waits=[tk["mean2"]])
            tk["var2"] = A("V", lambda e: e.scalar_tensor_tensor(
                stat[:, c2_:c2_ + 1], stat[:, c2_ + 1:c2_ + 2], 1.0 / D,
                stat[:, c2_ + 3:c2_ + 4], ALU.mult, ALU.subtract),
                waits=[tk["sq2"], tk["msq2"]])
            tk["ln2"] = A("S", lambda e: e.activation(
                stat[:, c2_ + 1:c2_ + 2], stat[:, c2_:c2_ + 1], AF.Ln,
                bias=consts[:, C_EPS:C_EPS + 1], scale=1.0), waits=[tk["var2"], t_k])
            tk["rstd2"] = A("S", lambda e: e.activation(
                stat[:, c2_:c2_ + 1], stat[:, c2_ + 1:c2_ + 2], AF.Exp,
                bias=0.0, scale=-0.5), waits=[tk["ln2"]])
        elif stg == 8:
            zs = st % 4
            tk["o"] = A("V", lambda e, zs=zs: e.tensor_scalar(
                xf[:, st, :], zf[:, zs, :], stat[:, c2_ + 2:c2_ + 3],
                stat[:, c2_:c2_ + 1], ALU.subtract, ALU.mult), waits=[tk["rstd2"]])
            rot_use["zf"][zs] = tk["o"]
            rot_use["st2"][par] = tk["o"]
            def stf(e, st=st):
                return e.dma_start(out=out_h[st * 128:(st + 1) * 128, :],
                                   in_=xf[:, st, :])
            out_toks.append(DM("SY", stf, waits=[tk["o"]], close=(st == NT - 1)))

    while wf_steps:
        wf_steps.pop(0)()
    rot_use["qs"][0] = wf_toks[-1]
    NSTG = 9
    for r in range(NT + NSTG - 1):
        for stg in reversed(range(NSTG)):
            st = r - stg
            if 0 <= st < NT:
                stage_fn(stg, st)

    A("SY", lambda e: e.nop(), waits=[out_toks[-1]], inc=False)

    from contextlib import ExitStack
    with ExitStack() as ctx:
        semT = ctx.enter_context(nc.semaphore("semT"))
        semS = ctx.enter_context(nc.semaphore("semS"))
        semV = ctx.enter_context(nc.semaphore("semV"))
        semSY = ctx.enter_context(nc.semaphore("semSY"))
        semG = ctx.enter_context(nc.semaphore("semG"))
        semGC = ctx.enter_context(nc.semaphore("semGC"))
        sems = {"T": semT, "S": semS, "V": semV, "SY": semSY, "G": semG, "GC": semGC}
        block = ctx.enter_context(nc.Block())

        @block.tensor
        def _(tensor):
            sched.replay("T", tensor, sems)

        @block.scalar
        def _(scalar):
            sched.replay("S", scalar, sems)

        @block.vector
        def _(vector):
            sched.replay("V", vector, sems)

        @block.sync
        def _(sync):
            sched.replay("SY", sync, sems)

        @block.gpsimd
        def _(gpsimd):
            sched.replay("G", gpsimd, sems)

    for cm in reversed(list(live_cms)):
        free(cm)
    psb4_cm.__exit__(None, None, None)
    ps1_cm.__exit__(None, None, None)
    return nc


def _host_prep(L0_attn_wq_w, L0_attn_wq_b, L0_attn_wk_w, L0_attn_wk_b,
               L0_attn_wv_w, L0_attn_wv_b, L0_ffn_weight, L0_ffn_bias):
    perm = np.zeros(D, dtype=np.int64)
    for dt in range(8):
        for p in range(128):
            if dt < 4:
                h = 4 * dt + p // 32
                chan = h * 64 + 2 * (p % 32)
            else:
                h = 4 * (dt - 4) + p // 32
                chan = h * 64 + 2 * (p % 32) + 1
            perm[dt * 128 + p] = chan
    wqT = np.ascontiguousarray(np.asarray(L0_attn_wq_w)[perm, :].T)
    wkT = np.ascontiguousarray(np.asarray(L0_attn_wk_w)[perm, :].T)
    wvT = np.ascontiguousarray(np.asarray(L0_attn_wv_w).T)
    wfT = np.ascontiguousarray(np.asarray(L0_ffn_weight).T)
    bq = np.ascontiguousarray(np.asarray(L0_attn_wq_b)[perm].reshape(8, 128).T)
    bk = np.ascontiguousarray(np.asarray(L0_attn_wk_b)[perm].reshape(8, 128).T)
    bvrep = np.ascontiguousarray(np.broadcast_to(np.asarray(L0_attn_wv_b)[None, :], (128, D)))
    bfrep = np.ascontiguousarray(np.broadcast_to(np.asarray(L0_ffn_bias)[None, :], (128, D)))
    i = np.arange(HD // 2, dtype=np.float32)
    theta = (np.float32(10000.0) ** (np.float32(-2.0) * i / np.float32(HD))).astype(np.float32)
    m = np.arange(S, dtype=np.float32)
    ang = np.outer(m, theta).astype(np.float32)
    cos = np.cos(ang).astype(np.float32)
    sin = np.sin(ang).astype(np.float32)
    crep = np.ascontiguousarray(np.tile(cos.T, (4, 1)))
    srep = np.ascontiguousarray(np.tile(sin.T, (4, 1)))
    eye32 = np.eye(128, dtype=np.float32)
    eyebf = np.eye(128, dtype=ml_dtypes.bfloat16)
    consts = np.zeros((128, 8), dtype=np.float32)
    consts[:, C_025] = 0.25
    consts[:, C_ILN2] = np.float32(1.0 / LN2)
    consts[:, C_LN2] = np.float32(LN2)
    consts[:, C_EPS] = np.float32(1e-5)
    consts[:, C_SHIFT] = np.float32(SHIFT)
    return dict(wqT=wqT, wkT=wkT, wvT=wvT, wfT=wfT, bq=bq, bk=bk,
                bvrep=bvrep, bfrep=bfrep, crep=crep, srep=srep,
                eye32=eye32, eyebf=eyebf, consts=consts)


def kernel(L0_attn_x, L0_attn_wq_w, L0_attn_wq_b, L0_attn_wk_w, L0_attn_wk_b,
           L0_attn_wv_w, L0_attn_wv_b, L0_ffn_weight, L0_ffn_bias):
    shared = _host_prep(L0_attn_wq_w, L0_attn_wq_b, L0_attn_wk_w, L0_attn_wk_b,
                        L0_attn_wv_w, L0_attn_wv_b, L0_ffn_weight, L0_ffn_bias)
    x = np.asarray(L0_attn_x, dtype=np.float32)
    nc = build_nc()
    in_maps = []
    for b in range(B):
        m = dict(shared)
        m["x"] = np.ascontiguousarray(x[b])
        in_maps.append(m)
    res = run_bass_kernel_spmd(nc, in_maps, core_ids=list(range(B)))
    out = np.stack([np.asarray(r["out"]) for r in res.results], axis=0)
    return out.astype(np.float32)


# version 15
# speedup vs baseline: 1.2030x; 1.2030x over previous
import numpy as np
import ml_dtypes
import concourse.bass as bass
import concourse.mybir as mybir
from concourse.bass_utils import run_bass_kernel_spmd

F32 = mybir.dt.float32
BF16 = mybir.dt.bfloat16
AF = mybir.ActivationFunctionType
ALU = mybir.AluOpType

B, S, D, H, HD = 8, 2048, 1024, 16, 64
NT = S // 128
NC = D // 128
LN2 = float(np.log(2.0))
SHIFT = -8.0

C_025, C_ILN2, C_LN2, C_EPS, C_SHIFT, C_ZERO = 0, 1, 2, 3, 4, 5


class Sched:

    DMA = {"SY", "G"}

    def __init__(self):
        self.items = []
        self.cnt = {"T": 0, "S": 0, "V": 0, "SY": 0, "G": 0, "GC": 0}
        self.batch_open = {"SY": False, "G": False}

    def add(self, eng, fn, waits=(), inc=True):
        waits = [w for w in waits if w is not None]
        if eng in self.DMA:
            self.items.append((eng, fn, waits, False))
            return None
        tok = None
        if inc:
            self.cnt[eng] += 1
            tok = (eng, self.cnt[eng])
        self.items.append((eng, fn, waits, inc))
        return tok

    def dma(self, eng, fn, waits=(), close=True):
        waits = [w for w in waits if w is not None]
        if not self.batch_open[eng]:
            if self.cnt[eng] > 0:
                waits = waits + [(eng, self.cnt[eng])]
            self.batch_open[eng] = True
        self.cnt[eng] += 16
        self.items.append((eng, fn, waits, True))
        if close:
            self.batch_open[eng] = False
            return (eng, self.cnt[eng])
        return None

    def barrier_tokens(self):
        return [(e, v) for e, v in self.cnt.items() if v > 0]

    def replay(self, eng_name, engine, sems):
        names = {"G": ("G", "GC")}.get(eng_name, (eng_name,))
        lastw = {}
        for eng, fn, waits, inc in self.items:
            if eng not in names:
                continue
            for we, wv in waits:
                if lastw.get(we, -1) >= wv:
                    continue
                engine.wait_ge(sems[we], wv)
                lastw[we] = wv
            inst = fn(engine)
            if inc:
                inst.then_inc(sems[eng], 16 if eng in self.DMA else 1)


def build_nc():
    nc = bass.Bass("TRN2", target_bir_lowering=False)

    def hin(name, shape, dt=F32):
        return nc.dram_tensor(name, list(shape), dt, kind="ExternalInput").ap()

    x_h = hin("x", (S, D))
    wq_h = hin("wqT", (D, D))
    wk_h = hin("wkT", (D, D))
    wv_h = hin("wvT", (D, D))
    wf_h = hin("wfT", (D, D))
    bq_h = hin("bq", (128, 8))
    bk_h = hin("bk", (128, 8))
    bv_h = hin("bvrep", (128, D))
    bf_h = hin("bfrep", (128, D))
    cos_h = hin("crep", (128, S))
    sin_h = hin("srep", (128, S))
    eye_h = hin("eye32", (128, 128))
    eyb_h = hin("eyebf", (128, 128), BF16)
    k_h = hin("consts", (128, 8))
    out_h = nc.dram_tensor("out", [S, D], F32, kind="ExternalOutput").ap()

    sched = Sched()
    A = sched.add

    live_cms = []

    def sb(name, shape, dt=F32, side="left"):
        cm = nc.sbuf_tensor(name, list(shape), dt, side=side)
        t = cm.__enter__()
        live_cms.append(cm)
        return t, cm

    def free(cm):
        cm.__exit__(None, None, None)
        live_cms.remove(cm)

    consts, _ = sb("k_s", (128, 8))
    eye32, _ = sb("eye32_s", (128, 128))
    eyebf, _ = sb("eyebf_s", (128, 128), BF16)
    bq_s, _ = sb("bq_s", (128, 8))
    bk_s, _ = sb("bk_s", (128, 8))
    cder, _ = sb("cder", (128, 4))
    rc_sb, _ = sb("rc_sb", (128, 16))
    stat, _ = sb("stat", (128, 24))
    ones1, _ = sb("ones1", (1, 128))

    qT, qT_cm = sb("qT", (128, NC, S), BF16)
    kT, kT_cm = sb("kT", (128, NC, S), BF16)
    va, va_cm = sb("va", (128, NT, H * 65), BF16)
    bv_s, bv_cm = sb("bv_s", (128, D), side="right")
    crep, crep_cm = sb("crep_s", (128, S), BF16, side="right")
    srep, srep_cm = sb("srep_s", (128, S), BF16, side="right")
    xT, xT_cm = sb("xT", (128, NC, S), BF16, side="right")
    wb, wb_cm = sb("wb", (128, 2, NC, D), BF16, side="right")
    xb16, xb16_cm = sb("xb16", (128, NT, D), BF16, side="right")

    psb0_cm = nc.psum_tensor("psb0", [128, 2048], BF16)
    psb0 = psb0_cm.__enter__()
    psb1_cm = nc.psum_tensor("psb1", [128, 2048], BF16)
    psb1 = psb1_cm.__enter__()

    DM = sched.dma
    DM("SY", lambda e: e.dma_start(out=consts[:], in_=k_h), close=False)
    DM("SY", lambda e: e.dma_start(out=eye32[:], in_=eye_h), close=False)
    DM("SY", lambda e: e.dma_start(out=eyebf[:], in_=eyb_h), close=False)
    DM("SY", lambda e: e.dma_start(out=bq_s[:], in_=bq_h), close=False)
    DM("SY", lambda e: e.dma_start(out=bk_s[:], in_=bk_h), close=False)
    t_P0 = DM("SY", lambda e: e.dma_start(out=bv_s[:], in_=bv_h), close=True)
    t_k = t_eye = t_eyb = t_bq = t_bk = t_bv = t_P0
    x3 = x_h.rearrange("(st p) d -> p st d", p=128)
    xb_toks = []
    for g4 in range(4):
        xb_toks.append(DM("G", lambda e, g4=g4: e.dma_start(
            out=xb16[:, g4 * 4:(g4 + 1) * 4, :], in_=x3[:, g4 * 4:(g4 + 1) * 4, :])))
    DM("G", lambda e: e.dma_start(out=crep[:], in_=cos_h), close=False)
    t_tab = DM("G", lambda e: e.dma_start(out=srep[:], in_=sin_h), close=True)
    t_cos = t_sin = t_tab
    wq3 = wq_h.rearrange("(ct p) d -> p ct d", p=128)
    t_wq = DM("G", lambda e: e.dma_start(out=wb[:, 0], in_=wq3))
    wk3 = wk_h.rearrange("(ct p) d -> p ct d", p=128)
    t_wk = DM("G", lambda e: e.dma_start(out=wb[:, 1], in_=wk3))
    wv3 = wv_h.rearrange("(ct p) d -> p ct d", p=128)

    t_p0 = A("V", lambda e: e.memset(cder[:, 2:3], 1.0))
    t_p1 = A("V", lambda e: e.tensor_scalar_add(cder[:, 3:4], cder[:, 2:3], 2.0 ** -30),
             waits=[t_p0])
    t_p2 = A("V", lambda e: e.tensor_scalar(cder[:, 2:3], cder[:, 3:4], 1.0, 2.0 ** 30,
                                            ALU.subtract, ALU.mult), waits=[t_p1])
    t_C0 = A("V", lambda e: e.tensor_scalar(cder[:, 0:1], cder[:, 2:3],
                                            1.5 * 2.0 ** 52 - 1.5 * 2.0 ** 23,
                                            1.5 * 2.0 ** 23, ALU.mult, ALU.add),
             waits=[t_p2])
    t_C67 = A("V", lambda e: e.tensor_scalar_add(cder[:, 1:2], cder[:, 0:1], 67.0),
              waits=[t_C0])
    t_ones = A("V", lambda e: e.memset(va[:], 1.0))
    t_ones1 = A("V", lambda e: e.memset(ones1[:], 1.0))

    tr_evac = {0: None, 1: None}
    last_tr = None
    for st in range(NT):
        par = st % 2
        ps = psb0 if par == 0 else psb1
        deps = [xb_toks[st // 4], t_eyb]
        if st >= 2:
            deps.append(tr_evac[par])
        for ct in range(NC):
            def trf(e, st=st, ct=ct, ps=ps):
                return e.transpose(ps[:, ct * 128:(ct + 1) * 128],
                                   xb16[:, st, ct * 128:(ct + 1) * 128], eyebf[:])
            last_tr = A("T", trf, waits=deps if ct == 0 else [], inc=(ct == NC - 1))
        def evf(e, st=st, ps=ps):
            srcp = ps.ap()[:, 0:1024].rearrange("p (ct c) -> p ct c", c=128)
            return e.tensor_copy(xT.ap()[:, :, st * 128:(st + 1) * 128], srcp)
        tr_evac[par] = A("V", evf, waits=[last_tr])
    t_xT = tr_evac[(NT - 1) % 2]
    psb1_cm.__exit__(None, None, None)
    psb0_cm.__exit__(None, None, None)
    ps1_cm = nc.psum_tensor("ps1", [128, 2048], F32)
    ps1 = ps1_cm.__enter__()
    ps0_cm = nc.psum_tensor("ps0", [128, 2048], F32)
    ps0 = ps0_cm.__enter__()

    free(xb16_cm)
    qraw, qraw_cm = sb("qraw", (128, 2, S), BF16, side="right")
    rs, rs_cm = sb("rs", (128, 4, 1024), BF16, side="right")
    ro, ro_cm = sb("ro", (128, 2, S), BF16, side="right")

    kq_weight_tok = {0: t_wq, 1: t_wk}
    proj_srcs = [(0, bq_s, qT), (1, bk_s, kT)]
    band_toks = []
    psum_free = {0: tr_evac[0], 1: tr_evac[1]}
    prev_rope_read = None
    prev_rope_read_g = None
    prev_band = {0: None, 1: None}
    w_last_use = None
    t_wv = None

    for pi, (pidx, bias_s, dstT) in enumerate(proj_srcs):
        t_w = kq_weight_tok[pidx]
        for g in range(4):
            pair_evacs = []
            for slot, dt in enumerate((g, g + 4)):
                idx = pi * 8 + g * 2 + slot
                par = idx % 2
                ps = ps0 if par == 0 else ps1
                mm_waits = [t_w, t_xT, psum_free.get(par)]
                last = None
                for cc in range(NC):
                    for nq in range(4):
                        def mmf(e, cc=cc, nq=nq, dt=dt, ps=ps, pi=pi):
                            return e.matmul(ps[:, nq * 512:(nq + 1) * 512],
                                            wb[:, pi, cc, dt * 128:(dt + 1) * 128],
                                            xT[:, cc, nq * 512:(nq + 1) * 512],
                                            start=(cc == 0), stop=(cc == NC - 1))
                        last = A("T", mmf,
                                 waits=mm_waits if (cc == 0 and nq == 0) else [],
                                 inc=(cc == NC - 1 and nq == 3))
                w_last_use = last
                ev_waits = [last, t_bq if pidx == 0 else t_bk, prev_rope_read,
                            prev_rope_read_g]
                def evf(e, slot=slot, ps=ps, bias_s=bias_s, dt=dt):
                    return e.tensor_scalar_add(qraw[:, slot, :], ps[:],
                                               bias_s[:, dt:dt + 1])
                tok = A("V", evf, waits=ev_waits)
                psum_free[par] = tok
                pair_evacs.append(tok)
            rope_deps = pair_evacs + [t_cos, t_sin,
                                      prev_band[pi] or prev_band[1 - pi]]
            v_done = None
            g_done = None
            for hh in range(2):
                sl = slice(hh * 1024, (hh + 1) * 1024)
                t1 = A("V", lambda e, sl=sl: e.tensor_mul(rs[:, 0, :], qraw[:, 0, sl], crep[:, sl]),
                       waits=rope_deps if hh == 0 else [v_done])
                t2 = A("V", lambda e, sl=sl: e.tensor_mul(rs[:, 1, :], qraw[:, 1, sl], srep[:, sl]))
                v_done = A("V", lambda e, sl=sl: e.tensor_sub(ro[:, 0, sl], rs[:, 0, :], rs[:, 1, :]),
                           waits=[t1, t2])
                g1 = A("GC", lambda e, sl=sl: e.tensor_mul(rs[:, 2, :], qraw[:, 0, sl], srep[:, sl]),
                       waits=rope_deps if hh == 0 else [g_done])
                g2 = A("GC", lambda e, sl=sl: e.tensor_mul(rs[:, 3, :], qraw[:, 1, sl], crep[:, sl]))
                g_done = A("GC", lambda e, sl=sl: e.tensor_add(ro[:, 1, sl], rs[:, 2, :], rs[:, 3, :]),
                           waits=[g1, g2])
            last_ro = v_done
            last_ro_g = g_done
            prev_rope_read = v_done
            prev_rope_read_g = g_done
            bt = None
            for a in range(4):
                h = 4 * g + a
                tt_ = h // 2
                base = (h % 2) * 64
                def bdr(e, a=a, tt_=tt_, base=base, dstT=dstT):
                    return e.dma_start(out=dstT[base:base + 32, tt_, :],
                                       in_=ro[a * 32:(a + 1) * 32, 0, :])
                def bdi(e, a=a, tt_=tt_, base=base, dstT=dstT):
                    return e.dma_start(out=dstT[base + 32:base + 64, tt_, :],
                                       in_=ro[a * 32:(a + 1) * 32, 1, :])
                DM("SY", bdr, waits=[last_ro, last_ro_g] if a == 0 else [], close=False)
                bt = DM("SY", bdi, close=(a == 3))
            prev_band[pi] = bt
            band_toks.append(bt)
        if pi == 0:
            t_wv = DM("G", lambda e: e.dma_start(out=wb[:, 0], in_=wv3),
                      waits=[w_last_use])

    last_v_evac = None
    for st in range(NT):
        par = st % 2
        ps = ps0 if par == 0 else ps1
        mm_waits = [t_wv, t_xT, psum_free.get(par)]
        last = None
        for cc in range(NC):
            for nv in range(2):
                def mmv(e, cc=cc, nv=nv, st=st, ps=ps):
                    return e.matmul(ps[:, nv * 512:(nv + 1) * 512],
                                    xT[:, cc, st * 128:(st + 1) * 128],
                                    wb[:, 0, cc, nv * 512:(nv + 1) * 512],
                                    start=(cc == 0), stop=(cc == NC - 1))
                last = A("T", mmv, waits=mm_waits if (cc == 0 and nv == 0) else [],
                         inc=(cc == NC - 1 and nv == 1))
        def evv(e, st=st, ps=ps):
            pa = ps.ap()[:, 0:1024].rearrange("p (h d) -> p h d", d=64)
            bva = bv_s.ap().rearrange("p (h d) -> p h d", d=64)
            dst = va.ap()[:, st, :].rearrange("p (h d) -> p h d", d=65)[:, :, 0:64]
            return e.tensor_add(dst, pa, bva)
        tok = A("V", evv, waits=[last, t_bv, t_ones])
        psum_free[par] = tok
        last_v_evac = tok

    free(ro_cm); free(rs_cm); free(qraw_cm); free(wb_cm)
    free(xT_cm); free(srep_cm); free(crep_cm); free(bv_cm)
    xf, xf_cm = sb("xf", (128, NT, D), side="right")
    wfs, wfs_cm = sb("wfs", (128, 1024), side="right")
    wfq, wfq_cm = sb("wfq", (128, NC, D), BF16, side="right")
    qs_t, qs_cm = sb("qsc", (128, 3, 1024), side="right")
    PT, PT_cm = sb("PT", (128, 2, 1024), BF16, side="right")
    oa_sb, oa_cm = sb("oa_sb", (128, 1024), side="right")

    ps0_cm.__exit__(None, None, None)
    ps_sc_cm = nc.psum_tensor("ps_sc", [128, 2, 1024], F32)
    ps_sc = ps_sc_cm.__enter__()

    bar2 = sched.barrier_tokens()

    t_xf = DM("SY", lambda e: e.dma_start(out=xf[:], in_=x3), waits=bar2)

    def quant_chain(src_fn, dst_fn, deps, extra_dst_waits=(), qs=None):
        if qs is None:
            qs = qs_t
        tA = A("S", lambda e: e.activation(qs[:, 0, :], src_fn(), AF.Abs,
                                           bias=0.0, scale=512.0), waits=list(deps))
        tL = A("S", lambda e: e.activation(qs[:, 1, :], qs[:, 0, :], AF.Ln,
                                           bias=consts[:, C_025:C_025 + 1], scale=1.0),
               waits=[tA, t_k])
        tM = A("V", lambda e: e.tensor_scalar(qs[:, 0, :], qs[:, 1, :],
                                              consts[:, C_ILN2:C_ILN2 + 1], 63.5,
                                              ALU.mult, ALU.add), waits=[tL, t_k])
        tE = A("V", lambda e: e.tensor_scalar(qs[:, 1, :], qs[:, 0, :],
                                              cder[:, 0:1], cder[:, 1:2],
                                              ALU.add, ALU.subtract), waits=[tM, t_C67])
        tS = A("S", lambda e: e.activation(qs[:, 0, :], qs[:, 1, :], AF.Exp,
                                           bias=0.0, scale=consts[:, C_LN2:C_LN2 + 1]),
               waits=[tE])
        tMt = A("V", lambda e: e.tensor_scalar(qs[:, 1, :], qs[:, 0, :], 1.0,
                                               cder[:, 0:1], ALU.max, ALU.mult),
                waits=[tS])
        tP = A("V", lambda e: e.scalar_tensor_tensor(qs[:, 0, :], src_fn(), 512.0,
                                                     qs[:, 1, :], ALU.mult, ALU.add),
               waits=[tMt] + list(deps))
        tQ = A("V", lambda e: e.tensor_sub(qs[:, 2, :], qs[:, 0, :], qs[:, 1, :]),
               waits=[tP])
        tZ = A("V", lambda e: e.tensor_scalar_mul(dst_fn(), qs[:, 2, :], 2.0 ** -9),
               waits=[tQ] + list(extra_dst_waits))
        return tZ

    wf3 = wf_h.rearrange("(ct p) d -> p ct d", p=128)
    wf_toks = []

    wf_steps = []

    def build_wf_steps():
        state = {}
        for cc in range(NC):
            def s_ld(cc=cc):
                ld_waits = list(bar2) if cc == 0 else [wf_toks[cc - 1]]
                state["ld"] = DM("G", lambda e, cc=cc: e.dma_start(
                    out=wfs[:], in_=wf3[:, cc, :]), waits=ld_waits)
            def s_A(cc=cc):
                state["A"] = A("S", lambda e: e.activation(
                    qs_t[:, 0, :], wfs[:], AF.Abs, bias=0.0, scale=512.0),
                    waits=[state["ld"]] + ([wf_toks[cc - 1]] if cc else []))
            def s_L(cc=cc):
                state["L"] = A("S", lambda e: e.activation(
                    qs_t[:, 1, :], qs_t[:, 0, :], AF.Ln,
                    bias=consts[:, C_025:C_025 + 1], scale=1.0),
                    waits=[state["A"], t_k])
            def s_M(cc=cc):
                state["M"] = A("V", lambda e: e.tensor_scalar(
                    qs_t[:, 0, :], qs_t[:, 1, :], consts[:, C_ILN2:C_ILN2 + 1],
                    63.5, ALU.mult, ALU.add), waits=[state["L"], t_k])
            def s_E(cc=cc):
                state["E"] = A("V", lambda e: e.tensor_scalar(
                    qs_t[:, 1, :], qs_t[:, 0, :], cder[:, 0:1], cder[:, 1:2],
                    ALU.add, ALU.subtract), waits=[state["M"], t_C67])
            def s_S(cc=cc):
                state["S"] = A("S", lambda e: e.activation(
                    qs_t[:, 0, :], qs_t[:, 1, :], AF.Exp, bias=0.0,
                    scale=consts[:, C_LN2:C_LN2 + 1]), waits=[state["E"]])
            def s_Mt(cc=cc):
                state["Mt"] = A("V", lambda e: e.tensor_scalar(
                    qs_t[:, 1, :], qs_t[:, 0, :], 1.0, cder[:, 0:1],
                    ALU.max, ALU.mult), waits=[state["S"]])
            def s_P(cc=cc):
                state["P"] = A("V", lambda e: e.scalar_tensor_tensor(
                    qs_t[:, 0, :], wfs[:], 512.0, qs_t[:, 1, :],
                    ALU.mult, ALU.add), waits=[state["Mt"]])
            def s_Q(cc=cc):
                state["Q"] = A("V", lambda e: e.tensor_sub(
                    qs_t[:, 2, :], qs_t[:, 0, :], qs_t[:, 1, :]), waits=[state["P"]])
            def s_Z(cc=cc):
                tz = A("V", lambda e, cc=cc: e.tensor_scalar_mul(
                    wfq[:, cc, :], qs_t[:, 2, :], 2.0 ** -9), waits=[state["Q"]])
                wf_toks.append(tz)
            wf_steps.extend([s_ld, s_A, s_L, s_M, s_E, s_S, s_Mt, s_P, s_Q, s_Z])

    build_wf_steps()

    qk_ready = band_toks[-1]
    va_ready = last_v_evac
    prev_div = None
    prev_ev = None
    prev_exp = None
    prev_pv = {}
    first_attn = [qk_ready, va_ready, psum_free[0], psum_free[1]] + bar2
    tile_map = [(t // 4) * 512 + (t % 4) * 65 for t in range(8)]
    attn_done = None

    exp_hist = {}
    prev_trp_ref = [None]
    prev_div_ref = [None]

    def attention_head(qc, hp, first):
        hA, hB = 2 * hp, 2 * hp + 1
        q0 = qc * 512

        def emit_scores(kt, carry):
            slot = kt % 2
            sc_waits = []
            if first and kt == 0:
                sc_waits += list(first_attn)
            sc_waits.append(exp_hist.get(kt - 2, carry))
            last_sc = None
            for half in range(2):
                r0 = half * 64
                def scf(e, half=half, r0=r0, kt=kt, hp=hp, q0=q0, slot=slot):
                    return e.matmul(
                        ps_sc[:, slot, half * 512:(half + 1) * 512],
                        kT[r0:r0 + 64, hp, kt * 128:(kt + 1) * 128],
                        qT[r0:r0 + 64, hp, q0:q0 + 512],
                        start=True, stop=True, tile_position=(r0, 0))
                last_sc = A("T", scf, waits=sc_waits if half == 0 else [],
                            inc=(half == 1))
            return last_sc

        def emit_exp(kt, last_sc):
            slot = kt % 2
            ex_waits = [last_sc, t_k, prev_pv.get(slot)]
            def exf(e, slot=slot):
                return e.activation(PT[:, slot, :], ps_sc[:, slot, :], AF.Exp,
                                    bias=consts[:, C_SHIFT:C_SHIFT + 1], scale=0.125)
            t_ex = A("S", exf, waits=ex_waits)
            exp_hist[kt] = t_ex
            return t_ex

        def emit_pv(kt, t_ex, pset):
            slot = kt % 2
            pv_waits = [t_ex]
            if kt == 0:
                pv_waits += [va_ready, div_done[pset]] + (list(bar2) if first else [])
            last_pv = None
            for half, h in ((0, hA), (1, hB)):
                def pvf(e, half=half, h=h, kt=kt, slot=slot, pset=pset):
                    base = pset * 1024 + half * 512
                    return e.matmul(
                        ps1[0:65, base:base + 512],
                        va[:, kt, h * 65: h * 65 + 65],
                        PT[:, slot, half * 512:(half + 1) * 512],
                        start=(kt == 0), stop=(kt == NT - 1))
                last_pv = A("T", pvf, waits=pv_waits if half == 0 else [],
                            inc=(half == 1))
            prev_pv[slot] = last_pv
            return last_pv

        return emit_scores, emit_exp, emit_pv, (hA, hB)

    def post_evac(hA, hB, qc, last_pv, pset):
        base = pset * 1024
        def eva(e, base=base):
            return e.tensor_copy(oa_sb[0:65, :], ps1[0:65, base:base + 1024])
        t_ev = A("V", eva, waits=[last_pv, trp_done[0], trp_done[1]])
        return (hA, hB, qc, pset, t_ev)

    def post_transpose(state, j, last):
        hA, hB, qc, pset, t_ev = state
        base = pset * 1024
        half, t = j // 4, j % 4
        col0 = base + half * 512 + t * 65
        def trf2(e, half=half, t=t, col0=col0):
            return e.transpose(ps1[:, col0:col0 + 65],
                               oa_sb[0:65, half * 512 + t * 128: half * 512 + (t + 1) * 128],
                               eye32[0:65, 0:65])
        return A("T", trf2, waits=[t_ev, t_eye] if j == 0 else [], inc=(j == 7))

    def post_finish(state, last_trp):
        hA, hB, qc, pset, t_ev = state
        base = pset * 1024
        trp_done[pset] = last_trp
        last_rc = None
        for j in range(8):
            half, t = j // 4, j % 4
            col0 = base + half * 512 + t * 65
            def rcf(e, j=j, col0=col0):
                return e.reciprocal(rc_sb[:, j:j + 1], ps1[:, col0 + 64:col0 + 65])
            last_rc = A("V", rcf,
                        waits=[last_trp, prev_div_ref[0]] if j == 0 else [],
                        inc=(j == 7))
        last_div = None
        for j in range(8):
            half, t = j // 4, j % 4
            h = hA if half == 0 else hB
            col0 = base + half * 512 + t * 65
            qtile = qc * 4 + t
            def dvf(e, h=h, j=j, col0=col0, qtile=qtile):
                xsl = xf[:, qtile, h * 64:(h + 1) * 64]
                return e.scalar_tensor_tensor(
                    xsl, ps1[:, col0:col0 + 64], rc_sb[:, j:j + 1],
                    xsl, ALU.mult, ALU.add)
            last_div = A("V", dvf, waits=[last_rc, t_xf] if j == 0 else [],
                         inc=(j == 7))
        div_done[pset] = last_div
        prev_div_ref[0] = last_div

    def emit_post(hA, hB, qc, last_pv, pset):
        state = post_evac(hA, hB, qc, last_pv, pset)
        last = None
        for j in range(8):
            last = post_transpose(state, j, last)
        post_finish(state, last)

    heads = [(qc, hp) for qc in range(4) for hp in range(8)]
    post_q = []
    div_done = {0: None, 1: None}
    trp_done = {0: None, 1: None}
    pending_pv = None
    for idx, (qc, hp) in enumerate(heads):
        emit_scores, emit_exp, emit_pv, (hA, hB) = attention_head(qc, hp, idx == 0)
        pset = idx % 2
        carry = prev_exp
        sc_tok = emit_scores(0, carry)
        if pending_pv is not None:
            p_emit, e14, e15, p_set, p_args = pending_pv
            p_emit(NT - 2, e14, p_set)
        emit_exp(0, sc_tok)
        sc_tok = emit_scores(1, carry)
        if pending_pv is not None:
            last_pv_prev = p_emit(NT - 1, e15, p_set)
            post_q.append(p_args + (last_pv_prev, p_set))
            pending_pv = None
        emit_exp(1, sc_tok)
        post_state = [None, None]
        for kt in range(2, NT):
            sc_tok = emit_scores(kt, carry)
            emit_pv(kt - 2, exp_hist[kt - 2], pset)
            emit_exp(kt, sc_tok)
            if kt in (4, 9, 13) and wf_steps and idx >= 4:
                wf_steps.pop(0)()
            if kt == 4 and post_q:
                post_state[0] = post_evac(*post_q.pop(0))
            if post_state[0] is not None and 5 <= kt <= 12:
                post_state[1] = post_transpose(post_state[0], kt - 5, post_state[1])
            if kt == 13 and post_state[0] is not None:
                post_finish(post_state[0], post_state[1])
                post_state = [None, None]
        pending_pv = (emit_pv, exp_hist[NT - 2], exp_hist[NT - 1], pset,
                      (hA, hB, qc))
        prev_exp = exp_hist[NT - 1]
        exp_hist = {}
    p_emit, e14, e15, p_set, p_args = pending_pv
    p_emit(NT - 2, e14, p_set)
    last_pv_prev = p_emit(NT - 1, e15, p_set)
    post_q.append(p_args + (last_pv_prev, p_set))
    while post_q:
        emit_post(*post_q.pop(0))
    attn_done = prev_div_ref[0]
    attn_done = prev_div_ref[0]

    free(oa_cm); free(PT_cm)
    free(va_cm); free(kT_cm); free(qT_cm)
    ps_sc_cm.__exit__(None, None, None)
    psb4_cm = nc.psum_tensor("psb4", [128, 2, 1024], BF16)
    psb4 = psb4_cm.__enter__()
    yqT, yqT_cm = sb("yqT", (128, NC, S), BF16)
    yqp, yqp_cm = sb("yqp", (128, 4, 1024), BF16)
    zf, zf_cm = sb("zf", (128, 4, 1024))
    bf_s, bf_cm = sb("bf_s", (128, D))
    qs2, qs2_cm = sb("qs2", (128, 3, 1024))
    dump_t, _dump_cm = sb("dump_t", (128, 1024))
    yf, yf_cm = sb("yf", (128, 5, 1024))
    bar3 = sched.barrier_tokens()
    t_bf = DM("SY", lambda e: e.dma_start(out=bf_s[:], in_=bf_h), waits=bar3)

    def layernorm(src_fn, dst_fn, deps, c0, dst_waits=()):
        tsum = A("S", lambda e: e.activation(wfs[:], src_fn(), AF.Copy,
                                             bias=0.0, scale=1.0,
                                             accum_out=stat[:, c0:c0 + 1]),
                 waits=list(deps) + [t_wfq])
        tsq = A("S", lambda e: e.activation(wfs[:], src_fn(), AF.Square,
                                            bias=0.0, scale=1.0,
                                            accum_out=stat[:, c0 + 1:c0 + 2]),
                waits=[tsum])
        tmean = A("V", lambda e: e.tensor_scalar_mul(stat[:, c0 + 2:c0 + 3],
                                                     stat[:, c0:c0 + 1], 1.0 / D),
                  waits=[tsum])
        tmsq = A("V", lambda e: e.tensor_mul(stat[:, c0 + 3:c0 + 4],
                                             stat[:, c0 + 2:c0 + 3],
                                             stat[:, c0 + 2:c0 + 3]), waits=[tmean])
        tvar = A("V", lambda e: e.scalar_tensor_tensor(
            stat[:, c0:c0 + 1], stat[:, c0 + 1:c0 + 2], 1.0 / D,
            stat[:, c0 + 3:c0 + 4], ALU.mult, ALU.subtract), waits=[tsq, tmsq])
        tln = A("S", lambda e: e.activation(stat[:, c0 + 1:c0 + 2], stat[:, c0:c0 + 1],
                                            AF.Ln, bias=consts[:, C_EPS:C_EPS + 1],
                                            scale=1.0), waits=[tvar, t_k])
        trstd = A("S", lambda e: e.activation(stat[:, c0:c0 + 1], stat[:, c0 + 1:c0 + 2],
                                              AF.Exp, bias=0.0, scale=-0.5), waits=[tln])
        ts = A("V", lambda e: e.tensor_scalar(dst_fn(), src_fn(),
                                              stat[:, c0 + 2:c0 + 3],
                                              stat[:, c0:c0 + 1],
                                              ALU.subtract, ALU.mult),
               waits=[trstd] + list(dst_waits))
        return ts

    st_tok = [dict() for _ in range(NT)]
    out_toks = []
    rot_use = {"yf": {}, "yqp": {}, "zf": {}, "qs": {}, "psb4": {}, "ffn": {},
               "st1": {}, "st2": {}, "dump": [None]}

    def stage_fn(stg, st):
        tk = st_tok[st]
        par = st % 2
        c0 = (st % 4) * 4
        c2_ = 16 + par * 4
        if stg == 0:
            deps = [attn_done, rot_use["st1"].get(st % 4)]
            tk["sum"] = A("V", lambda e: e.reduce_sum(stat[:, c0:c0 + 1], xf[:, st, :],
                                                      axis=mybir.AxisListType.X),
                          waits=deps)
            tk["sq"] = A("S", lambda e: e.activation(dump_t[:], xf[:, st, :], AF.Square,
                                                     bias=0.0, scale=1.0,
                                                     accum_out=stat[:, c0 + 1:c0 + 2]),
                         waits=deps + [rot_use["dump"][0]])
            rot_use["dump"][0] = tk["sq"]
        elif stg == 1:
            tk["mean"] = A("V", lambda e: e.tensor_scalar_mul(stat[:, c0 + 2:c0 + 3],
                                                              stat[:, c0:c0 + 1], 1.0 / D),
                           waits=[tk["sum"]])
            tk["msq"] = A("V", lambda e: e.tensor_mul(stat[:, c0 + 3:c0 + 4],
                                                      stat[:, c0 + 2:c0 + 3],
                                                      stat[:, c0 + 2:c0 + 3]),
                          waits=[tk["mean"]])
            tk["var"] = A("V", lambda e: e.scalar_tensor_tensor(
                stat[:, c0:c0 + 1], stat[:, c0 + 1:c0 + 2], 1.0 / D,
                stat[:, c0 + 3:c0 + 4], ALU.mult, ALU.subtract),
                waits=[tk["sq"], tk["msq"]])
            tk["ln"] = A("S", lambda e: e.activation(stat[:, c0 + 1:c0 + 2],
                                                     stat[:, c0:c0 + 1], AF.Ln,
                                                     bias=consts[:, C_EPS:C_EPS + 1],
                                                     scale=1.0), waits=[tk["var"], t_k])
            tk["rstd"] = A("S", lambda e: e.activation(stat[:, c0:c0 + 1],
                                                       stat[:, c0 + 1:c0 + 2], AF.Exp,
                                                       bias=0.0, scale=-0.5),
                           waits=[tk["ln"]])
        elif stg == 2:
            ys = st % 5
            tk["y"] = A("V", lambda e: e.tensor_scalar(yf[:, ys, :], xf[:, st, :],
                                                       stat[:, c0 + 2:c0 + 3],
                                                       stat[:, c0:c0 + 1],
                                                       ALU.subtract, ALU.mult),
                        waits=[tk["rstd"], rot_use["yf"].get(ys)])
            rot_use["st1"][st % 4] = tk["y"]
            qs = qs_t if par == 0 else qs2
            tk["qA"] = A("S", lambda e: e.activation(qs[:, 0, :], yf[:, ys, :], AF.Abs,
                                                     bias=0.0, scale=512.0),
                         waits=[tk["y"], rot_use["qs"].get(par)])
            tk["qL"] = A("S", lambda e: e.activation(qs[:, 1, :], qs[:, 0, :], AF.Ln,
                                                     bias=consts[:, C_025:C_025 + 1],
                                                     scale=1.0), waits=[tk["qA"]])
        elif stg == 3:
            qs = qs_t if par == 0 else qs2
            tk["qM"] = A("V", lambda e: e.tensor_scalar(qs[:, 0, :], qs[:, 1, :],
                                                        consts[:, C_ILN2:C_ILN2 + 1],
                                                        63.5, ALU.mult, ALU.add),
                         waits=[tk["qL"], t_k])
            tk["qE"] = A("V", lambda e: e.tensor_scalar(qs[:, 1, :], qs[:, 0, :],
                                                        cder[:, 0:1], cder[:, 1:2],
                                                        ALU.add, ALU.subtract),
                         waits=[tk["qM"], t_C67])
            tk["qS"] = A("S", lambda e: e.activation(qs[:, 0, :], qs[:, 1, :], AF.Exp,
                                                     bias=0.0,
                                                     scale=consts[:, C_LN2:C_LN2 + 1]),
                         waits=[tk["qE"]])
        elif stg == 4:
            qs = qs_t if par == 0 else qs2
            ys = st % 5
            tk["qMt"] = A("V", lambda e: e.tensor_scalar(qs[:, 1, :], qs[:, 0, :], 1.0,
                                                         cder[:, 0:1], ALU.max, ALU.mult),
                          waits=[tk["qS"]])
            tk["qP"] = A("V", lambda e: e.scalar_tensor_tensor(qs[:, 0, :], yf[:, ys, :],
                                                               512.0, qs[:, 1, :],
                                                               ALU.mult, ALU.add),
                         waits=[tk["qMt"]])
            tk["qQ"] = A("V", lambda e: e.tensor_sub(qs[:, 2, :], qs[:, 0, :],
                                                     qs[:, 1, :]), waits=[tk["qP"]])
            qp = st % 4
            tk["qZ"] = A("S", lambda e: e.mul(yqp[:, qp, :], qs[:, 2, :], 2.0 ** -9),
                         waits=[tk["qQ"], rot_use["yqp"].get(qp)])
            rot_use["qs"][par] = tk["qZ"]
        elif stg == 5:
            qp = st % 4
            last_tq = None
            tr_waits = [tk["qZ"], rot_use["psb4"].get(par)]
            for ctt in range(NC):
                def tqf(e, ctt=ctt, par=par, qp=qp):
                    return e.transpose(psb4[:, par, ctt * 128:(ctt + 1) * 128],
                                       yqp[:, qp, ctt * 128:(ctt + 1) * 128], eyebf[:])
                last_tq = A("T", tqf, waits=tr_waits if ctt == 0 else [],
                            inc=(ctt == NC - 1))
            rot_use["yqp"][qp] = last_tq
            def tqe(e, st=st, par=par):
                srcp = psb4.ap()[:, par, :].rearrange("p (ct c) -> p ct c", c=128)
                dst = yqT.ap()[:, :, st * 128:(st + 1) * 128]
                return e.activation(dst, srcp, AF.Copy, bias=0.0, scale=1.0)
            tk["tqe"] = A("S", tqe, waits=[last_tq])
            rot_use["psb4"][par] = tk["tqe"]
        elif stg == 6:
            poff = par * 1024
            mm_waits = [tk["tqe"], rot_use["ffn"].get(par)]
            last = None
            for cc in range(NC):
                mm_waits_cc = [wf_toks[cc]] + (mm_waits if cc == 0 else [])
                for c2 in range(2):
                    def mf(e, cc=cc, c2=c2, st=st, poff=poff):
                        return e.matmul(ps1[:, poff + c2 * 512: poff + (c2 + 1) * 512],
                                        yqT[:, cc, st * 128:(st + 1) * 128],
                                        wfq[:, cc, c2 * 512:(c2 + 1) * 512],
                                        start=(cc == 0), stop=False)
                    last = A("T", mf, waits=mm_waits_cc if c2 == 0 else [], inc=False)
            for c2 in range(2):
                def mb(e, c2=c2, poff=poff):
                    return e.matmul(ps1[:, poff + c2 * 512: poff + (c2 + 1) * 512],
                                    ones1[0:1, :], bf_s[0:1, c2 * 512:(c2 + 1) * 512],
                                    start=False, stop=True)
                last = A("T", mb, waits=[t_bf, t_ones1] if c2 == 0 else [],
                         inc=(c2 == 1))
            zs = st % 4
            ys = st % 5
            tk["z2"] = A("V", lambda e, zs=zs, ys=ys, poff=poff: e.tensor_add(
                zf[:, zs, :], ps1[:, poff:poff + 1024], yf[:, ys, :]),
                waits=[last, rot_use["zf"].get(zs)])
            rot_use["ffn"][par] = tk["z2"]
            rot_use["yf"][ys] = tk["z2"]
        elif stg == 7:
            zs = st % 4
            deps = [tk["z2"], rot_use["st2"].get(par)]
            tk["sum2"] = A("V", lambda e, zs=zs: e.reduce_sum(
                stat[:, c2_:c2_ + 1], zf[:, zs, :], axis=mybir.AxisListType.X),
                waits=deps)
            tk["sq2"] = A("S", lambda e, zs=zs: e.activation(
                dump_t[:], zf[:, zs, :], AF.Square, bias=0.0, scale=1.0,
                accum_out=stat[:, c2_ + 1:c2_ + 2]),
                waits=deps + [rot_use["dump"][0]])
            rot_use["dump"][0] = tk["sq2"]
            tk["mean2"] = A("V", lambda e: e.tensor_scalar_mul(
                stat[:, c2_ + 2:c2_ + 3], stat[:, c2_:c2_ + 1], 1.0 / D),
                waits=[tk["sum2"]])
            tk["msq2"] = A("V", lambda e: e.tensor_mul(
                stat[:, c2_ + 3:c2_ + 4], stat[:, c2_ + 2:c2_ + 3],
                stat[:, c2_ + 2:c2_ + 3]), waits=[tk["mean2"]])
            tk["var2"] = A("V", lambda e: e.scalar_tensor_tensor(
                stat[:, c2_:c2_ + 1], stat[:, c2_ + 1:c2_ + 2], 1.0 / D,
                stat[:, c2_ + 3:c2_ + 4], ALU.mult, ALU.subtract),
                waits=[tk["sq2"], tk["msq2"]])
            tk["ln2"] = A("S", lambda e: e.activation(
                stat[:, c2_ + 1:c2_ + 2], stat[:, c2_:c2_ + 1], AF.Ln,
                bias=consts[:, C_EPS:C_EPS + 1], scale=1.0), waits=[tk["var2"], t_k])
            tk["rstd2"] = A("S", lambda e: e.activation(
                stat[:, c2_:c2_ + 1], stat[:, c2_ + 1:c2_ + 2], AF.Exp,
                bias=0.0, scale=-0.5), waits=[tk["ln2"]])
        elif stg == 8:
            zs = st % 4
            tk["o"] = A("V", lambda e, zs=zs: e.tensor_scalar(
                xf[:, st, :], zf[:, zs, :], stat[:, c2_ + 2:c2_ + 3],
                stat[:, c2_:c2_ + 1], ALU.subtract, ALU.mult), waits=[tk["rstd2"]])
            rot_use["zf"][zs] = tk["o"]
            rot_use["st2"][par] = tk["o"]
            def stf(e, st=st):
                return e.dma_start(out=out_h[st * 128:(st + 1) * 128, :],
                                   in_=xf[:, st, :])
            out_toks.append(DM("SY", stf, waits=[tk["o"]], close=(st == NT - 1)))

    while wf_steps:
        wf_steps.pop(0)()
    rot_use["qs"][0] = wf_toks[-1]
    NSTG = 9
    for r in range(NT + NSTG - 1):
        for stg in reversed(range(NSTG)):
            st = r - stg
            if 0 <= st < NT:
                stage_fn(stg, st)

    A("SY", lambda e: e.nop(), waits=[out_toks[-1]], inc=False)

    from contextlib import ExitStack
    with ExitStack() as ctx:
        semT = ctx.enter_context(nc.semaphore("semT"))
        semS = ctx.enter_context(nc.semaphore("semS"))
        semV = ctx.enter_context(nc.semaphore("semV"))
        semSY = ctx.enter_context(nc.semaphore("semSY"))
        semG = ctx.enter_context(nc.semaphore("semG"))
        semGC = ctx.enter_context(nc.semaphore("semGC"))
        sems = {"T": semT, "S": semS, "V": semV, "SY": semSY, "G": semG, "GC": semGC}
        block = ctx.enter_context(nc.Block())

        @block.tensor
        def _(tensor):
            sched.replay("T", tensor, sems)

        @block.scalar
        def _(scalar):
            sched.replay("S", scalar, sems)

        @block.vector
        def _(vector):
            sched.replay("V", vector, sems)

        @block.sync
        def _(sync):
            sched.replay("SY", sync, sems)

        @block.gpsimd
        def _(gpsimd):
            sched.replay("G", gpsimd, sems)

    for cm in reversed(list(live_cms)):
        free(cm)
    psb4_cm.__exit__(None, None, None)
    ps1_cm.__exit__(None, None, None)
    return nc


def _host_prep(L0_attn_wq_w, L0_attn_wq_b, L0_attn_wk_w, L0_attn_wk_b,
               L0_attn_wv_w, L0_attn_wv_b, L0_ffn_weight, L0_ffn_bias):
    perm = np.zeros(D, dtype=np.int64)
    for dt in range(8):
        for p in range(128):
            if dt < 4:
                h = 4 * dt + p // 32
                chan = h * 64 + 2 * (p % 32)
            else:
                h = 4 * (dt - 4) + p // 32
                chan = h * 64 + 2 * (p % 32) + 1
            perm[dt * 128 + p] = chan
    wqT = np.ascontiguousarray(np.asarray(L0_attn_wq_w)[perm, :].T)
    wkT = np.ascontiguousarray(np.asarray(L0_attn_wk_w)[perm, :].T)
    wvT = np.ascontiguousarray(np.asarray(L0_attn_wv_w).T)
    wfT = np.ascontiguousarray(np.asarray(L0_ffn_weight).T)
    bq = np.ascontiguousarray(np.asarray(L0_attn_wq_b)[perm].reshape(8, 128).T)
    bk = np.ascontiguousarray(np.asarray(L0_attn_wk_b)[perm].reshape(8, 128).T)
    bvrep = np.ascontiguousarray(np.broadcast_to(np.asarray(L0_attn_wv_b)[None, :], (128, D)))
    bfrep = np.ascontiguousarray(np.broadcast_to(np.asarray(L0_ffn_bias)[None, :], (128, D)))
    i = np.arange(HD // 2, dtype=np.float32)
    theta = (np.float32(10000.0) ** (np.float32(-2.0) * i / np.float32(HD))).astype(np.float32)
    m = np.arange(S, dtype=np.float32)
    ang = np.outer(m, theta).astype(np.float32)
    cos = np.cos(ang).astype(np.float32)
    sin = np.sin(ang).astype(np.float32)
    crep = np.ascontiguousarray(np.tile(cos.T, (4, 1)))
    srep = np.ascontiguousarray(np.tile(sin.T, (4, 1)))
    eye32 = np.eye(128, dtype=np.float32)
    eyebf = np.eye(128, dtype=ml_dtypes.bfloat16)
    consts = np.zeros((128, 8), dtype=np.float32)
    consts[:, C_025] = 0.25
    consts[:, C_ILN2] = np.float32(1.0 / LN2)
    consts[:, C_LN2] = np.float32(LN2)
    consts[:, C_EPS] = np.float32(1e-5)
    consts[:, C_SHIFT] = np.float32(SHIFT)
    return dict(wqT=wqT, wkT=wkT, wvT=wvT, wfT=wfT, bq=bq, bk=bk,
                bvrep=bvrep, bfrep=bfrep, crep=crep, srep=srep,
                eye32=eye32, eyebf=eyebf, consts=consts)


def kernel(L0_attn_x, L0_attn_wq_w, L0_attn_wq_b, L0_attn_wk_w, L0_attn_wk_b,
           L0_attn_wv_w, L0_attn_wv_b, L0_ffn_weight, L0_ffn_bias):
    shared = _host_prep(L0_attn_wq_w, L0_attn_wq_b, L0_attn_wk_w, L0_attn_wk_b,
                        L0_attn_wv_w, L0_attn_wv_b, L0_ffn_weight, L0_ffn_bias)
    x = np.asarray(L0_attn_x, dtype=np.float32)
    nc = build_nc()
    in_maps = []
    for b in range(B):
        m = dict(shared)
        m["x"] = np.ascontiguousarray(x[b])
        in_maps.append(m)
    res = run_bass_kernel_spmd(nc, in_maps, core_ids=list(range(B)))
    out = np.stack([np.asarray(r["out"]) for r in res.results], axis=0)
    return out.astype(np.float32)


# version 16
# speedup vs baseline: 1.2159x; 1.0107x over previous
import numpy as np
import ml_dtypes
import concourse.bass as bass
import concourse.mybir as mybir
from concourse.bass_utils import run_bass_kernel_spmd

F32 = mybir.dt.float32
BF16 = mybir.dt.bfloat16
AF = mybir.ActivationFunctionType
ALU = mybir.AluOpType

B, S, D, H, HD = 8, 2048, 1024, 16, 64
NT = S // 128
NC = D // 128
LN2 = float(np.log(2.0))
SHIFT = -8.0

C_025, C_ILN2, C_LN2, C_EPS, C_SHIFT, C_ZERO = 0, 1, 2, 3, 4, 5


class Sched:

    DMA = {"SY", "G"}

    def __init__(self):
        self.items = []
        self.cnt = {"T": 0, "S": 0, "V": 0, "SY": 0, "G": 0, "GC": 0}
        self.batch_open = {"SY": False, "G": False}

    def add(self, eng, fn, waits=(), inc=True):
        waits = [w for w in waits if w is not None]
        if eng in self.DMA:
            self.items.append((eng, fn, waits, False))
            return None
        tok = None
        if inc:
            self.cnt[eng] += 1
            tok = (eng, self.cnt[eng])
        self.items.append((eng, fn, waits, inc))
        return tok

    def dma(self, eng, fn, waits=(), close=True):
        waits = [w for w in waits if w is not None]
        if not self.batch_open[eng]:
            if self.cnt[eng] > 0:
                waits = waits + [(eng, self.cnt[eng])]
            self.batch_open[eng] = True
        self.cnt[eng] += 16
        self.items.append((eng, fn, waits, True))
        if close:
            self.batch_open[eng] = False
            return (eng, self.cnt[eng])
        return None

    def barrier_tokens(self):
        return [(e, v) for e, v in self.cnt.items() if v > 0]

    def replay(self, eng_name, engine, sems):
        names = {"G": ("G", "GC")}.get(eng_name, (eng_name,))
        lastw = {}
        for eng, fn, waits, inc in self.items:
            if eng not in names:
                continue
            for we, wv in waits:
                if lastw.get(we, -1) >= wv:
                    continue
                engine.wait_ge(sems[we], wv)
                lastw[we] = wv
            inst = fn(engine)
            if inc:
                inst.then_inc(sems[eng], 16 if eng in self.DMA else 1)


def build_nc():
    nc = bass.Bass("TRN2", target_bir_lowering=False)

    def hin(name, shape, dt=F32):
        return nc.dram_tensor(name, list(shape), dt, kind="ExternalInput").ap()

    x_h = hin("x", (S, D))
    wq_h = hin("wqT", (D, D))
    wk_h = hin("wkT", (D, D))
    wv_h = hin("wvT", (D, D))
    wf_h = hin("wfT", (D, D))
    bq_h = hin("bq", (128, 8))
    bk_h = hin("bk", (128, 8))
    bv_h = hin("bvrep", (128, D))
    bf_h = hin("bfrep", (128, D))
    cos_h = hin("crep", (128, S))
    sin_h = hin("srep", (128, S))
    eye_h = hin("eye32", (128, 128))
    eyb_h = hin("eyebf", (128, 128), BF16)
    k_h = hin("consts", (128, 8))
    out_h = nc.dram_tensor("out", [S, D], F32, kind="ExternalOutput").ap()

    sched = Sched()
    A = sched.add

    live_cms = []

    def sb(name, shape, dt=F32, side="left"):
        cm = nc.sbuf_tensor(name, list(shape), dt, side=side)
        t = cm.__enter__()
        live_cms.append(cm)
        return t, cm

    def free(cm):
        cm.__exit__(None, None, None)
        live_cms.remove(cm)

    consts, _ = sb("k_s", (128, 8))
    eye32, _ = sb("eye32_s", (128, 128))
    eyebf, _ = sb("eyebf_s", (128, 128), BF16)
    bq_s, _ = sb("bq_s", (128, 8))
    bk_s, _ = sb("bk_s", (128, 8))
    cder, _ = sb("cder", (128, 4))
    rc_sb, _ = sb("rc_sb", (128, 16))
    stat, _ = sb("stat", (128, 24))
    ones1, _ = sb("ones1", (1, 128))

    qT, qT_cm = sb("qT", (128, NC, S), BF16)
    kT, kT_cm = sb("kT", (128, NC, S), BF16)
    va, va_cm = sb("va", (128, NT, H * 65), BF16)
    bv_s, bv_cm = sb("bv_s", (128, D), side="right")
    crep, crep_cm = sb("crep_s", (128, S), BF16, side="right")
    srep, srep_cm = sb("srep_s", (128, S), BF16, side="right")
    xT, xT_cm = sb("xT", (128, NC, S), BF16, side="right")
    wb, wb_cm = sb("wb", (128, 2, NC, D), BF16, side="right")
    xb16, xb16_cm = sb("xb16", (128, NT, D), BF16, side="right")

    psb0_cm = nc.psum_tensor("psb0", [128, 2048], BF16)
    psb0 = psb0_cm.__enter__()
    psb1_cm = nc.psum_tensor("psb1", [128, 2048], BF16)
    psb1 = psb1_cm.__enter__()

    DM = sched.dma
    DM("SY", lambda e: e.dma_start(out=consts[:], in_=k_h), close=False)
    DM("SY", lambda e: e.dma_start(out=eye32[:], in_=eye_h), close=False)
    DM("SY", lambda e: e.dma_start(out=eyebf[:], in_=eyb_h), close=False)
    DM("SY", lambda e: e.dma_start(out=bq_s[:], in_=bq_h), close=False)
    DM("SY", lambda e: e.dma_start(out=bk_s[:], in_=bk_h), close=False)
    t_P0 = DM("SY", lambda e: e.dma_start(out=bv_s[:], in_=bv_h), close=True)
    t_k = t_eye = t_eyb = t_bq = t_bk = t_bv = t_P0
    x3 = x_h.rearrange("(st p) d -> p st d", p=128)
    wq3 = wq_h.rearrange("(ct p) d -> p ct d", p=128)
    wk3 = wk_h.rearrange("(ct p) d -> p ct d", p=128)
    wv3 = wv_h.rearrange("(ct p) d -> p ct d", p=128)
    xb_toks = []
    xb_toks.append(DM("G", lambda e: e.dma_start(
        out=xb16[:, 0:4, :], in_=x3[:, 0:4, :])))
    t_wq = DM("G", lambda e: e.dma_start(out=wb[:, 0], in_=wq3))
    for g4 in range(1, 4):
        xb_toks.append(DM("G", lambda e, g4=g4: e.dma_start(
            out=xb16[:, g4 * 4:(g4 + 1) * 4, :], in_=x3[:, g4 * 4:(g4 + 1) * 4, :])))
    t_wk = DM("G", lambda e: e.dma_start(out=wb[:, 1], in_=wk3))
    DM("G", lambda e: e.dma_start(out=crep[:], in_=cos_h), close=False)
    t_tab = DM("G", lambda e: e.dma_start(out=srep[:], in_=sin_h), close=True)
    t_cos = t_sin = t_tab

    t_p0 = A("V", lambda e: e.memset(cder[:, 2:3], 1.0))
    t_p1 = A("V", lambda e: e.tensor_scalar_add(cder[:, 3:4], cder[:, 2:3], 2.0 ** -30),
             waits=[t_p0])
    t_p2 = A("V", lambda e: e.tensor_scalar(cder[:, 2:3], cder[:, 3:4], 1.0, 2.0 ** 30,
                                            ALU.subtract, ALU.mult), waits=[t_p1])
    t_C0 = A("V", lambda e: e.tensor_scalar(cder[:, 0:1], cder[:, 2:3],
                                            1.5 * 2.0 ** 52 - 1.5 * 2.0 ** 23,
                                            1.5 * 2.0 ** 23, ALU.mult, ALU.add),
             waits=[t_p2])
    t_C67 = A("V", lambda e: e.tensor_scalar_add(cder[:, 1:2], cder[:, 0:1], 67.0),
              waits=[t_C0])
    t_ones = A("V", lambda e: e.memset(va[:], 1.0))
    t_ones1 = A("V", lambda e: e.memset(ones1[:], 1.0))

    tr_evac = {0: None, 1: None}
    last_tr = None
    for st in range(NT):
        par = st % 2
        ps = psb0 if par == 0 else psb1
        deps = [xb_toks[st // 4], t_eyb]
        if st >= 2:
            deps.append(tr_evac[par])
        for ct in range(NC):
            def trf(e, st=st, ct=ct, ps=ps):
                return e.transpose(ps[:, ct * 128:(ct + 1) * 128],
                                   xb16[:, st, ct * 128:(ct + 1) * 128], eyebf[:])
            last_tr = A("T", trf, waits=deps if ct == 0 else [], inc=(ct == NC - 1))
        def evf(e, st=st, ps=ps):
            srcp = ps.ap()[:, 0:1024].rearrange("p (ct c) -> p ct c", c=128)
            return e.tensor_copy(xT.ap()[:, :, st * 128:(st + 1) * 128], srcp)
        tr_evac[par] = A("V", evf, waits=[last_tr])
    t_xT = tr_evac[(NT - 1) % 2]
    psb1_cm.__exit__(None, None, None)
    psb0_cm.__exit__(None, None, None)
    ps1_cm = nc.psum_tensor("ps1", [128, 2048], F32)
    ps1 = ps1_cm.__enter__()
    ps0_cm = nc.psum_tensor("ps0", [128, 2048], F32)
    ps0 = ps0_cm.__enter__()

    free(xb16_cm)
    qraw, qraw_cm = sb("qraw", (128, 2, S), BF16, side="right")
    rs, rs_cm = sb("rs", (128, 4, 1024), BF16, side="right")
    ro, ro_cm = sb("ro", (128, 2, S), BF16, side="right")

    kq_weight_tok = {0: t_wq, 1: t_wk}
    proj_srcs = [(0, bq_s, qT), (1, bk_s, kT)]
    band_toks = []
    psum_free = {0: tr_evac[0], 1: tr_evac[1]}
    prev_rope_read = None
    prev_rope_read_g = None
    prev_band = {0: None, 1: None}
    w_last_use = None
    t_wv = None

    for pi, (pidx, bias_s, dstT) in enumerate(proj_srcs):
        t_w = kq_weight_tok[pidx]
        for g in range(4):
            pair_evacs = []
            for slot, dt in enumerate((g, g + 4)):
                idx = pi * 8 + g * 2 + slot
                par = idx % 2
                ps = ps0 if par == 0 else ps1
                mm_waits = [t_w, t_xT, psum_free.get(par)]
                last = None
                for cc in range(NC):
                    for nq in range(4):
                        def mmf(e, cc=cc, nq=nq, dt=dt, ps=ps, pi=pi):
                            return e.matmul(ps[:, nq * 512:(nq + 1) * 512],
                                            wb[:, pi, cc, dt * 128:(dt + 1) * 128],
                                            xT[:, cc, nq * 512:(nq + 1) * 512],
                                            start=(cc == 0), stop=(cc == NC - 1))
                        last = A("T", mmf,
                                 waits=mm_waits if (cc == 0 and nq == 0) else [],
                                 inc=(cc == NC - 1 and nq == 3))
                w_last_use = last
                ev_waits = [last, t_bq if pidx == 0 else t_bk, prev_rope_read,
                            prev_rope_read_g]
                def evf(e, slot=slot, ps=ps, bias_s=bias_s, dt=dt):
                    return e.tensor_scalar_add(qraw[:, slot, :], ps[:],
                                               bias_s[:, dt:dt + 1])
                tok = A("V", evf, waits=ev_waits)
                psum_free[par] = tok
                pair_evacs.append(tok)
            rope_deps = pair_evacs + [t_cos, t_sin,
                                      prev_band[pi] or prev_band[1 - pi]]
            v_done = None
            g_done = None
            for hh in range(2):
                sl = slice(hh * 1024, (hh + 1) * 1024)
                t1 = A("V", lambda e, sl=sl: e.tensor_mul(rs[:, 0, :], qraw[:, 0, sl], crep[:, sl]),
                       waits=rope_deps if hh == 0 else [v_done])
                t2 = A("V", lambda e, sl=sl: e.tensor_mul(rs[:, 1, :], qraw[:, 1, sl], srep[:, sl]))
                v_done = A("V", lambda e, sl=sl: e.tensor_sub(ro[:, 0, sl], rs[:, 0, :], rs[:, 1, :]),
                           waits=[t1, t2])
                g1 = A("GC", lambda e, sl=sl: e.tensor_mul(rs[:, 2, :], qraw[:, 0, sl], srep[:, sl]),
                       waits=rope_deps if hh == 0 else [g_done])
                g2 = A("GC", lambda e, sl=sl: e.tensor_mul(rs[:, 3, :], qraw[:, 1, sl], crep[:, sl]))
                g_done = A("GC", lambda e, sl=sl: e.tensor_add(ro[:, 1, sl], rs[:, 2, :], rs[:, 3, :]),
                           waits=[g1, g2])
            last_ro = v_done
            last_ro_g = g_done
            prev_rope_read = v_done
            prev_rope_read_g = g_done
            bt = None
            for a in range(4):
                h = 4 * g + a
                tt_ = h // 2
                base = (h % 2) * 64
                def bdr(e, a=a, tt_=tt_, base=base, dstT=dstT):
                    return e.dma_start(out=dstT[base:base + 32, tt_, :],
                                       in_=ro[a * 32:(a + 1) * 32, 0, :])
                def bdi(e, a=a, tt_=tt_, base=base, dstT=dstT):
                    return e.dma_start(out=dstT[base + 32:base + 64, tt_, :],
                                       in_=ro[a * 32:(a + 1) * 32, 1, :])
                DM("SY", bdr, waits=[last_ro, last_ro_g] if a == 0 else [], close=False)
                bt = DM("SY", bdi, close=(a == 3))
            prev_band[pi] = bt
            band_toks.append(bt)
        if pi == 0:
            t_wv = DM("G", lambda e: e.dma_start(out=wb[:, 0], in_=wv3),
                      waits=[w_last_use])

    last_v_evac = None
    for st in range(NT):
        par = st % 2
        ps = ps0 if par == 0 else ps1
        mm_waits = [t_wv, t_xT, psum_free.get(par)]
        last = None
        for cc in range(NC):
            for nv in range(2):
                def mmv(e, cc=cc, nv=nv, st=st, ps=ps):
                    return e.matmul(ps[:, nv * 512:(nv + 1) * 512],
                                    xT[:, cc, st * 128:(st + 1) * 128],
                                    wb[:, 0, cc, nv * 512:(nv + 1) * 512],
                                    start=(cc == 0), stop=(cc == NC - 1))
                last = A("T", mmv, waits=mm_waits if (cc == 0 and nv == 0) else [],
                         inc=(cc == NC - 1 and nv == 1))
        def evv(e, st=st, ps=ps):
            pa = ps.ap()[:, 0:1024].rearrange("p (h d) -> p h d", d=64)
            bva = bv_s.ap().rearrange("p (h d) -> p h d", d=64)
            dst = va.ap()[:, st, :].rearrange("p (h d) -> p h d", d=65)[:, :, 0:64]
            return e.tensor_add(dst, pa, bva)
        tok = A("V", evv, waits=[last, t_bv, t_ones])
        psum_free[par] = tok
        last_v_evac = tok

    free(ro_cm); free(rs_cm); free(qraw_cm); free(wb_cm)
    free(xT_cm); free(srep_cm); free(crep_cm); free(bv_cm)
    xf, xf_cm = sb("xf", (128, NT, D), side="right")
    wfs, wfs_cm = sb("wfs", (128, 1024), side="right")
    wfq, wfq_cm = sb("wfq", (128, NC, D), BF16, side="right")
    qs_t, qs_cm = sb("qsc", (128, 3, 1024), side="right")
    PT, PT_cm = sb("PT", (128, 2, 1024), BF16, side="right")
    oa_sb, oa_cm = sb("oa_sb", (128, 1024), side="right")

    ps0_cm.__exit__(None, None, None)
    ps_sc_cm = nc.psum_tensor("ps_sc", [128, 2, 1024], F32)
    ps_sc = ps_sc_cm.__enter__()

    bar2 = sched.barrier_tokens()

    t_xf = DM("SY", lambda e: e.dma_start(out=xf[:], in_=x3), waits=bar2)

    def quant_chain(src_fn, dst_fn, deps, extra_dst_waits=(), qs=None):
        if qs is None:
            qs = qs_t
        tA = A("S", lambda e: e.activation(qs[:, 0, :], src_fn(), AF.Abs,
                                           bias=0.0, scale=512.0), waits=list(deps))
        tL = A("S", lambda e: e.activation(qs[:, 1, :], qs[:, 0, :], AF.Ln,
                                           bias=consts[:, C_025:C_025 + 1], scale=1.0),
               waits=[tA, t_k])
        tM = A("V", lambda e: e.tensor_scalar(qs[:, 0, :], qs[:, 1, :],
                                              consts[:, C_ILN2:C_ILN2 + 1], 63.5,
                                              ALU.mult, ALU.add), waits=[tL, t_k])
        tE = A("V", lambda e: e.tensor_scalar(qs[:, 1, :], qs[:, 0, :],
                                              cder[:, 0:1], cder[:, 1:2],
                                              ALU.add, ALU.subtract), waits=[tM, t_C67])
        tS = A("S", lambda e: e.activation(qs[:, 0, :], qs[:, 1, :], AF.Exp,
                                           bias=0.0, scale=consts[:, C_LN2:C_LN2 + 1]),
               waits=[tE])
        tMt = A("V", lambda e: e.tensor_scalar(qs[:, 1, :], qs[:, 0, :], 1.0,
                                               cder[:, 0:1], ALU.max, ALU.mult),
                waits=[tS])
        tP = A("V", lambda e: e.scalar_tensor_tensor(qs[:, 0, :], src_fn(), 512.0,
                                                     qs[:, 1, :], ALU.mult, ALU.add),
               waits=[tMt] + list(deps))
        tQ = A("V", lambda e: e.tensor_sub(qs[:, 2, :], qs[:, 0, :], qs[:, 1, :]),
               waits=[tP])
        tZ = A("V", lambda e: e.tensor_scalar_mul(dst_fn(), qs[:, 2, :], 2.0 ** -9),
               waits=[tQ] + list(extra_dst_waits))
        return tZ

    wf3 = wf_h.rearrange("(ct p) d -> p ct d", p=128)
    wf_toks = []

    wf_steps = []

    def build_wf_steps():
        state = {}
        for cc in range(NC):
            def s_ld(cc=cc):
                ld_waits = list(bar2) if cc == 0 else [wf_toks[cc - 1]]
                state["ld"] = DM("G", lambda e, cc=cc: e.dma_start(
                    out=wfs[:], in_=wf3[:, cc, :]), waits=ld_waits)
            def s_A(cc=cc):
                state["A"] = A("S", lambda e: e.activation(
                    qs_t[:, 0, :], wfs[:], AF.Abs, bias=0.0, scale=512.0),
                    waits=[state["ld"]] + ([wf_toks[cc - 1]] if cc else []))
            def s_L(cc=cc):
                state["L"] = A("S", lambda e: e.activation(
                    qs_t[:, 1, :], qs_t[:, 0, :], AF.Ln,
                    bias=consts[:, C_025:C_025 + 1], scale=1.0),
                    waits=[state["A"], t_k])
            def s_M(cc=cc):
                state["M"] = A("V", lambda e: e.tensor_scalar(
                    qs_t[:, 0, :], qs_t[:, 1, :], consts[:, C_ILN2:C_ILN2 + 1],
                    63.5, ALU.mult, ALU.add), waits=[state["L"], t_k])
            def s_E(cc=cc):
                state["E"] = A("V", lambda e: e.tensor_scalar(
                    qs_t[:, 1, :], qs_t[:, 0, :], cder[:, 0:1], cder[:, 1:2],
                    ALU.add, ALU.subtract), waits=[state["M"], t_C67])
            def s_S(cc=cc):
                state["S"] = A("S", lambda e: e.activation(
                    qs_t[:, 0, :], qs_t[:, 1, :], AF.Exp, bias=0.0,
                    scale=consts[:, C_LN2:C_LN2 + 1]), waits=[state["E"]])
            def s_Mt(cc=cc):
                state["Mt"] = A("V", lambda e: e.tensor_scalar(
                    qs_t[:, 1, :], qs_t[:, 0, :], 1.0, cder[:, 0:1],
                    ALU.max, ALU.mult), waits=[state["S"]])
            def s_P(cc=cc):
                state["P"] = A("V", lambda e: e.scalar_tensor_tensor(
                    qs_t[:, 0, :], wfs[:], 512.0, qs_t[:, 1, :],
                    ALU.mult, ALU.add), waits=[state["Mt"]])
            def s_Q(cc=cc):
                state["Q"] = A("V", lambda e: e.tensor_sub(
                    qs_t[:, 2, :], qs_t[:, 0, :], qs_t[:, 1, :]), waits=[state["P"]])
            def s_Z(cc=cc):
                tz = A("V", lambda e, cc=cc: e.tensor_scalar_mul(
                    wfq[:, cc, :], qs_t[:, 2, :], 2.0 ** -9), waits=[state["Q"]])
                wf_toks.append(tz)
            wf_steps.extend([s_ld, s_A, s_L, s_M, s_E, s_S, s_Mt, s_P, s_Q, s_Z])

    build_wf_steps()

    qk_ready = band_toks[-1]
    va_ready = last_v_evac
    prev_div = None
    prev_ev = None
    prev_exp = None
    prev_pv = {}
    first_attn = [qk_ready, va_ready, psum_free[0], psum_free[1]] + bar2
    tile_map = [(t // 4) * 512 + (t % 4) * 65 for t in range(8)]
    attn_done = None

    exp_hist = {}
    prev_trp_ref = [None]
    prev_div_ref = [None]

    def attention_head(qc, hp, first):
        hA, hB = 2 * hp, 2 * hp + 1
        q0 = qc * 512

        def emit_scores(kt, carry):
            slot = kt % 2
            sc_waits = []
            if first and kt == 0:
                sc_waits += list(first_attn)
            sc_waits.append(exp_hist.get(kt - 2, carry))
            last_sc = None
            for half in range(2):
                r0 = half * 64
                def scf(e, half=half, r0=r0, kt=kt, hp=hp, q0=q0, slot=slot):
                    return e.matmul(
                        ps_sc[:, slot, half * 512:(half + 1) * 512],
                        kT[r0:r0 + 64, hp, kt * 128:(kt + 1) * 128],
                        qT[r0:r0 + 64, hp, q0:q0 + 512],
                        start=True, stop=True, tile_position=(r0, 0))
                last_sc = A("T", scf, waits=sc_waits if half == 0 else [],
                            inc=(half == 1))
            return last_sc

        def emit_exp(kt, last_sc):
            slot = kt % 2
            ex_waits = [last_sc, t_k, prev_pv.get(slot)]
            def exf(e, slot=slot):
                return e.activation(PT[:, slot, :], ps_sc[:, slot, :], AF.Exp,
                                    bias=consts[:, C_SHIFT:C_SHIFT + 1], scale=0.125)
            t_ex = A("S", exf, waits=ex_waits)
            exp_hist[kt] = t_ex
            return t_ex

        def emit_pv(kt, t_ex, pset):
            slot = kt % 2
            pv_waits = [t_ex]
            if kt == 0:
                pv_waits += [va_ready, div_done[pset]] + (list(bar2) if first else [])
            last_pv = None
            for half, h in ((0, hA), (1, hB)):
                def pvf(e, half=half, h=h, kt=kt, slot=slot, pset=pset):
                    base = pset * 1024 + half * 512
                    return e.matmul(
                        ps1[0:65, base:base + 512],
                        va[:, kt, h * 65: h * 65 + 65],
                        PT[:, slot, half * 512:(half + 1) * 512],
                        start=(kt == 0), stop=(kt == NT - 1))
                last_pv = A("T", pvf, waits=pv_waits if half == 0 else [],
                            inc=(half == 1))
            prev_pv[slot] = last_pv
            return last_pv

        return emit_scores, emit_exp, emit_pv, (hA, hB)

    def post_evac(hA, hB, qc, last_pv, pset):
        base = pset * 1024
        def eva(e, base=base):
            return e.tensor_copy(oa_sb[0:65, :], ps1[0:65, base:base + 1024])
        t_ev = A("V", eva, waits=[last_pv, trp_done[0], trp_done[1]])
        return (hA, hB, qc, pset, t_ev)

    def post_transpose(state, j, last):
        hA, hB, qc, pset, t_ev = state
        base = pset * 1024
        half, t = j // 4, j % 4
        col0 = base + half * 512 + t * 65
        def trf2(e, half=half, t=t, col0=col0):
            return e.transpose(ps1[:, col0:col0 + 65],
                               oa_sb[0:65, half * 512 + t * 128: half * 512 + (t + 1) * 128],
                               eye32[0:65, 0:65])
        return A("T", trf2, waits=[t_ev, t_eye] if j == 0 else [], inc=(j == 7))

    def post_finish(state, last_trp):
        hA, hB, qc, pset, t_ev = state
        base = pset * 1024
        trp_done[pset] = last_trp
        last_rc = None
        for j in range(8):
            half, t = j // 4, j % 4
            col0 = base + half * 512 + t * 65
            def rcf(e, j=j, col0=col0):
                return e.reciprocal(rc_sb[:, j:j + 1], ps1[:, col0 + 64:col0 + 65])
            last_rc = A("V", rcf,
                        waits=[last_trp, prev_div_ref[0]] if j == 0 else [],
                        inc=(j == 7))
        last_div = None
        for j in range(8):
            half, t = j // 4, j % 4
            h = hA if half == 0 else hB
            col0 = base + half * 512 + t * 65
            qtile = qc * 4 + t
            def dvf(e, h=h, j=j, col0=col0, qtile=qtile):
                xsl = xf[:, qtile, h * 64:(h + 1) * 64]
                return e.scalar_tensor_tensor(
                    xsl, ps1[:, col0:col0 + 64], rc_sb[:, j:j + 1],
                    xsl, ALU.mult, ALU.add)
            last_div = A("V", dvf, waits=[last_rc, t_xf] if j == 0 else [],
                         inc=(j == 7))
        div_done[pset] = last_div
        prev_div_ref[0] = last_div

    def emit_post(hA, hB, qc, last_pv, pset):
        state = post_evac(hA, hB, qc, last_pv, pset)
        last = None
        for j in range(8):
            last = post_transpose(state, j, last)
        post_finish(state, last)

    heads = [(qc, hp) for qc in range(4) for hp in range(8)]
    post_q = []
    div_done = {0: None, 1: None}
    trp_done = {0: None, 1: None}
    pending_pv = None
    for idx, (qc, hp) in enumerate(heads):
        emit_scores, emit_exp, emit_pv, (hA, hB) = attention_head(qc, hp, idx == 0)
        pset = idx % 2
        carry = prev_exp
        sc_tok = emit_scores(0, carry)
        if pending_pv is not None:
            p_emit, e14, e15, p_set, p_args = pending_pv
            p_emit(NT - 2, e14, p_set)
        emit_exp(0, sc_tok)
        sc_tok = emit_scores(1, carry)
        if pending_pv is not None:
            last_pv_prev = p_emit(NT - 1, e15, p_set)
            post_q.append(p_args + (last_pv_prev, p_set))
            pending_pv = None
        emit_exp(1, sc_tok)
        post_state = [None, None]
        for kt in range(2, NT):
            sc_tok = emit_scores(kt, carry)
            emit_pv(kt - 2, exp_hist[kt - 2], pset)
            emit_exp(kt, sc_tok)
            if kt in (4, 9, 13) and wf_steps and idx >= 4:
                wf_steps.pop(0)()
            if kt == 4 and post_q:
                post_state[0] = post_evac(*post_q.pop(0))
            if post_state[0] is not None and 5 <= kt <= 12:
                post_state[1] = post_transpose(post_state[0], kt - 5, post_state[1])
            if kt == 13 and post_state[0] is not None:
                post_finish(post_state[0], post_state[1])
                post_state = [None, None]
        pending_pv = (emit_pv, exp_hist[NT - 2], exp_hist[NT - 1], pset,
                      (hA, hB, qc))
        prev_exp = exp_hist[NT - 1]
        exp_hist = {}
    p_emit, e14, e15, p_set, p_args = pending_pv
    p_emit(NT - 2, e14, p_set)
    last_pv_prev = p_emit(NT - 1, e15, p_set)
    post_q.append(p_args + (last_pv_prev, p_set))
    while post_q:
        emit_post(*post_q.pop(0))
    attn_done = prev_div_ref[0]
    attn_done = prev_div_ref[0]

    free(oa_cm); free(PT_cm)
    free(va_cm); free(kT_cm); free(qT_cm)
    ps_sc_cm.__exit__(None, None, None)
    psb4_cm = nc.psum_tensor("psb4", [128, 2, 1024], BF16)
    psb4 = psb4_cm.__enter__()
    yqT, yqT_cm = sb("yqT", (128, NC, S), BF16)
    yqp, yqp_cm = sb("yqp", (128, 4, 1024), BF16)
    zf, zf_cm = sb("zf", (128, 4, 1024))
    bf_s, bf_cm = sb("bf_s", (128, D))
    qs2, qs2_cm = sb("qs2", (128, 3, 1024))
    dump_t, _dump_cm = sb("dump_t", (128, 1024))
    yf, yf_cm = sb("yf", (128, 5, 1024))
    bar3 = sched.barrier_tokens()
    t_bf = DM("SY", lambda e: e.dma_start(out=bf_s[:], in_=bf_h), waits=bar3)

    def layernorm(src_fn, dst_fn, deps, c0, dst_waits=()):
        tsum = A("S", lambda e: e.activation(wfs[:], src_fn(), AF.Copy,
                                             bias=0.0, scale=1.0,
                                             accum_out=stat[:, c0:c0 + 1]),
                 waits=list(deps) + [t_wfq])
        tsq = A("S", lambda e: e.activation(wfs[:], src_fn(), AF.Square,
                                            bias=0.0, scale=1.0,
                                            accum_out=stat[:, c0 + 1:c0 + 2]),
                waits=[tsum])
        tmean = A("V", lambda e: e.tensor_scalar_mul(stat[:, c0 + 2:c0 + 3],
                                                     stat[:, c0:c0 + 1], 1.0 / D),
                  waits=[tsum])
        tmsq = A("V", lambda e: e.tensor_mul(stat[:, c0 + 3:c0 + 4],
                                             stat[:, c0 + 2:c0 + 3],
                                             stat[:, c0 + 2:c0 + 3]), waits=[tmean])
        tvar = A("V", lambda e: e.scalar_tensor_tensor(
            stat[:, c0:c0 + 1], stat[:, c0 + 1:c0 + 2], 1.0 / D,
            stat[:, c0 + 3:c0 + 4], ALU.mult, ALU.subtract), waits=[tsq, tmsq])
        tln = A("S", lambda e: e.activation(stat[:, c0 + 1:c0 + 2], stat[:, c0:c0 + 1],
                                            AF.Ln, bias=consts[:, C_EPS:C_EPS + 1],
                                            scale=1.0), waits=[tvar, t_k])
        trstd = A("S", lambda e: e.activation(stat[:, c0:c0 + 1], stat[:, c0 + 1:c0 + 2],
                                              AF.Exp, bias=0.0, scale=-0.5), waits=[tln])
        ts = A("V", lambda e: e.tensor_scalar(dst_fn(), src_fn(),
                                              stat[:, c0 + 2:c0 + 3],
                                              stat[:, c0:c0 + 1],
                                              ALU.subtract, ALU.mult),
               waits=[trstd] + list(dst_waits))
        return ts

    st_tok = [dict() for _ in range(NT)]
    out_toks = []
    rot_use = {"yf": {}, "yqp": {}, "zf": {}, "qs": {}, "psb4": {}, "ffn": {},
               "st1": {}, "st2": {}, "dump": [None]}

    def stage_fn(stg, st):
        tk = st_tok[st]
        par = st % 2
        c0 = (st % 4) * 4
        c2_ = 16 + par * 4
        if stg == 0:
            deps = [attn_done, rot_use["st1"].get(st % 4)]
            tk["sum"] = A("V", lambda e: e.reduce_sum(stat[:, c0:c0 + 1], xf[:, st, :],
                                                      axis=mybir.AxisListType.X),
                          waits=deps)
            tk["sq"] = A("S", lambda e: e.activation(dump_t[:], xf[:, st, :], AF.Square,
                                                     bias=0.0, scale=1.0,
                                                     accum_out=stat[:, c0 + 1:c0 + 2]),
                         waits=deps + [rot_use["dump"][0]])
            rot_use["dump"][0] = tk["sq"]
        elif stg == 1:
            tk["mean"] = A("V", lambda e: e.tensor_scalar_mul(stat[:, c0 + 2:c0 + 3],
                                                              stat[:, c0:c0 + 1], 1.0 / D),
                           waits=[tk["sum"]])
            tk["msq"] = A("V", lambda e: e.tensor_mul(stat[:, c0 + 3:c0 + 4],
                                                      stat[:, c0 + 2:c0 + 3],
                                                      stat[:, c0 + 2:c0 + 3]),
                          waits=[tk["mean"]])
            tk["var"] = A("V", lambda e: e.scalar_tensor_tensor(
                stat[:, c0:c0 + 1], stat[:, c0 + 1:c0 + 2], 1.0 / D,
                stat[:, c0 + 3:c0 + 4], ALU.mult, ALU.subtract),
                waits=[tk["sq"], tk["msq"]])
            tk["ln"] = A("S", lambda e: e.activation(stat[:, c0 + 1:c0 + 2],
                                                     stat[:, c0:c0 + 1], AF.Ln,
                                                     bias=consts[:, C_EPS:C_EPS + 1],
                                                     scale=1.0), waits=[tk["var"], t_k])
            tk["rstd"] = A("S", lambda e: e.activation(stat[:, c0:c0 + 1],
                                                       stat[:, c0 + 1:c0 + 2], AF.Exp,
                                                       bias=0.0, scale=-0.5),
                           waits=[tk["ln"]])
        elif stg == 2:
            ys = st % 5
            tk["y"] = A("V", lambda e: e.tensor_scalar(yf[:, ys, :], xf[:, st, :],
                                                       stat[:, c0 + 2:c0 + 3],
                                                       stat[:, c0:c0 + 1],
                                                       ALU.subtract, ALU.mult),
                        waits=[tk["rstd"], rot_use["yf"].get(ys)])
            rot_use["st1"][st % 4] = tk["y"]
            qs = qs_t if par == 0 else qs2
            tk["qA"] = A("S", lambda e: e.activation(qs[:, 0, :], yf[:, ys, :], AF.Abs,
                                                     bias=0.0, scale=512.0),
                         waits=[tk["y"], rot_use["qs"].get(par)])
            tk["qL"] = A("S", lambda e: e.activation(qs[:, 1, :], qs[:, 0, :], AF.Ln,
                                                     bias=consts[:, C_025:C_025 + 1],
                                                     scale=1.0), waits=[tk["qA"]])
        elif stg == 3:
            qs = qs_t if par == 0 else qs2
            tk["qM"] = A("V", lambda e: e.tensor_scalar(qs[:, 0, :], qs[:, 1, :],
                                                        consts[:, C_ILN2:C_ILN2 + 1],
                                                        63.5, ALU.mult, ALU.add),
                         waits=[tk["qL"], t_k])
            tk["qE"] = A("V", lambda e: e.tensor_scalar(qs[:, 1, :], qs[:, 0, :],
                                                        cder[:, 0:1], cder[:, 1:2],
                                                        ALU.add, ALU.subtract),
                         waits=[tk["qM"], t_C67])
            tk["qS"] = A("S", lambda e: e.activation(qs[:, 0, :], qs[:, 1, :], AF.Exp,
                                                     bias=0.0,
                                                     scale=consts[:, C_LN2:C_LN2 + 1]),
                         waits=[tk["qE"]])
        elif stg == 4:
            qs = qs_t if par == 0 else qs2
            ys = st % 5
            tk["qMt"] = A("V", lambda e: e.tensor_scalar(qs[:, 1, :], qs[:, 0, :], 1.0,
                                                         cder[:, 0:1], ALU.max, ALU.mult),
                          waits=[tk["qS"]])
            tk["qP"] = A("V", lambda e: e.scalar_tensor_tensor(qs[:, 0, :], yf[:, ys, :],
                                                               512.0, qs[:, 1, :],
                                                               ALU.mult, ALU.add),
                         waits=[tk["qMt"]])
            tk["qQ"] = A("V", lambda e: e.tensor_sub(qs[:, 2, :], qs[:, 0, :],
                                                     qs[:, 1, :]), waits=[tk["qP"]])
            qp = st % 4
            tk["qZ"] = A("S", lambda e: e.mul(yqp[:, qp, :], qs[:, 2, :], 2.0 ** -9),
                         waits=[tk["qQ"], rot_use["yqp"].get(qp)])
            rot_use["qs"][par] = tk["qZ"]
        elif stg == 5:
            qp = st % 4
            last_tq = None
            tr_waits = [tk["qZ"], rot_use["psb4"].get(par)]
            for ctt in range(NC):
                def tqf(e, ctt=ctt, par=par, qp=qp):
                    return e.transpose(psb4[:, par, ctt * 128:(ctt + 1) * 128],
                                       yqp[:, qp, ctt * 128:(ctt + 1) * 128], eyebf[:])
                last_tq = A("T", tqf, waits=tr_waits if ctt == 0 else [],
                            inc=(ctt == NC - 1))
            rot_use["yqp"][qp] = last_tq
            def tqe(e, st=st, par=par):
                srcp = psb4.ap()[:, par, :].rearrange("p (ct c) -> p ct c", c=128)
                dst = yqT.ap()[:, :, st * 128:(st + 1) * 128]
                return e.activation(dst, srcp, AF.Copy, bias=0.0, scale=1.0)
            tk["tqe"] = A("S", tqe, waits=[last_tq])
            rot_use["psb4"][par] = tk["tqe"]
        elif stg == 6:
            poff = par * 1024
            mm_waits = [tk["tqe"], rot_use["ffn"].get(par)]
            last = None
            for cc in range(NC):
                mm_waits_cc = [wf_toks[cc]] + (mm_waits if cc == 0 else [])
                for c2 in range(2):
                    def mf(e, cc=cc, c2=c2, st=st, poff=poff):
                        return e.matmul(ps1[:, poff + c2 * 512: poff + (c2 + 1) * 512],
                                        yqT[:, cc, st * 128:(st + 1) * 128],
                                        wfq[:, cc, c2 * 512:(c2 + 1) * 512],
                                        start=(cc == 0), stop=False)
                    last = A("T", mf, waits=mm_waits_cc if c2 == 0 else [], inc=False)
            for c2 in range(2):
                def mb(e, c2=c2, poff=poff):
                    return e.matmul(ps1[:, poff + c2 * 512: poff + (c2 + 1) * 512],
                                    ones1[0:1, :], bf_s[0:1, c2 * 512:(c2 + 1) * 512],
                                    start=False, stop=True)
                last = A("T", mb, waits=[t_bf, t_ones1] if c2 == 0 else [],
                         inc=(c2 == 1))
            zs = st % 4
            ys = st % 5
            tk["z2"] = A("V", lambda e, zs=zs, ys=ys, poff=poff: e.tensor_add(
                zf[:, zs, :], ps1[:, poff:poff + 1024], yf[:, ys, :]),
                waits=[last, rot_use["zf"].get(zs)])
            rot_use["ffn"][par] = tk["z2"]
            rot_use["yf"][ys] = tk["z2"]
        elif stg == 7:
            zs = st % 4
            deps = [tk["z2"], rot_use["st2"].get(par)]
            tk["sum2"] = A("V", lambda e, zs=zs: e.reduce_sum(
                stat[:, c2_:c2_ + 1], zf[:, zs, :], axis=mybir.AxisListType.X),
                waits=deps)
            tk["sq2"] = A("S", lambda e, zs=zs: e.activation(
                dump_t[:], zf[:, zs, :], AF.Square, bias=0.0, scale=1.0,
                accum_out=stat[:, c2_ + 1:c2_ + 2]),
                waits=deps + [rot_use["dump"][0]])
            rot_use["dump"][0] = tk["sq2"]
            tk["mean2"] = A("V", lambda e: e.tensor_scalar_mul(
                stat[:, c2_ + 2:c2_ + 3], stat[:, c2_:c2_ + 1], 1.0 / D),
                waits=[tk["sum2"]])
            tk["msq2"] = A("V", lambda e: e.tensor_mul(
                stat[:, c2_ + 3:c2_ + 4], stat[:, c2_ + 2:c2_ + 3],
                stat[:, c2_ + 2:c2_ + 3]), waits=[tk["mean2"]])
            tk["var2"] = A("V", lambda e: e.scalar_tensor_tensor(
                stat[:, c2_:c2_ + 1], stat[:, c2_ + 1:c2_ + 2], 1.0 / D,
                stat[:, c2_ + 3:c2_ + 4], ALU.mult, ALU.subtract),
                waits=[tk["sq2"], tk["msq2"]])
            tk["ln2"] = A("S", lambda e: e.activation(
                stat[:, c2_ + 1:c2_ + 2], stat[:, c2_:c2_ + 1], AF.Ln,
                bias=consts[:, C_EPS:C_EPS + 1], scale=1.0), waits=[tk["var2"], t_k])
            tk["rstd2"] = A("S", lambda e: e.activation(
                stat[:, c2_:c2_ + 1], stat[:, c2_ + 1:c2_ + 2], AF.Exp,
                bias=0.0, scale=-0.5), waits=[tk["ln2"]])
        elif stg == 8:
            zs = st % 4
            tk["o"] = A("V", lambda e, zs=zs: e.tensor_scalar(
                xf[:, st, :], zf[:, zs, :], stat[:, c2_ + 2:c2_ + 3],
                stat[:, c2_:c2_ + 1], ALU.subtract, ALU.mult), waits=[tk["rstd2"]])
            rot_use["zf"][zs] = tk["o"]
            rot_use["st2"][par] = tk["o"]
            def stf(e, st=st):
                return e.dma_start(out=out_h[st * 128:(st + 1) * 128, :],
                                   in_=xf[:, st, :])
            out_toks.append(DM("SY", stf, waits=[tk["o"]], close=(st == NT - 1)))

    while wf_steps:
        wf_steps.pop(0)()
    rot_use["qs"][0] = wf_toks[-1]
    NSTG = 9
    for r in range(NT + NSTG - 1):
        for stg in reversed(range(NSTG)):
            st = r - stg
            if 0 <= st < NT:
                stage_fn(stg, st)

    A("SY", lambda e: e.nop(), waits=[out_toks[-1]], inc=False)

    from contextlib import ExitStack
    with ExitStack() as ctx:
        semT = ctx.enter_context(nc.semaphore("semT"))
        semS = ctx.enter_context(nc.semaphore("semS"))
        semV = ctx.enter_context(nc.semaphore("semV"))
        semSY = ctx.enter_context(nc.semaphore("semSY"))
        semG = ctx.enter_context(nc.semaphore("semG"))
        semGC = ctx.enter_context(nc.semaphore("semGC"))
        sems = {"T": semT, "S": semS, "V": semV, "SY": semSY, "G": semG, "GC": semGC}
        block = ctx.enter_context(nc.Block())

        @block.tensor
        def _(tensor):
            sched.replay("T", tensor, sems)

        @block.scalar
        def _(scalar):
            sched.replay("S", scalar, sems)

        @block.vector
        def _(vector):
            sched.replay("V", vector, sems)

        @block.sync
        def _(sync):
            sched.replay("SY", sync, sems)

        @block.gpsimd
        def _(gpsimd):
            sched.replay("G", gpsimd, sems)

    for cm in reversed(list(live_cms)):
        free(cm)
    psb4_cm.__exit__(None, None, None)
    ps1_cm.__exit__(None, None, None)
    return nc


def _host_prep(L0_attn_wq_w, L0_attn_wq_b, L0_attn_wk_w, L0_attn_wk_b,
               L0_attn_wv_w, L0_attn_wv_b, L0_ffn_weight, L0_ffn_bias):
    perm = np.zeros(D, dtype=np.int64)
    for dt in range(8):
        for p in range(128):
            if dt < 4:
                h = 4 * dt + p // 32
                chan = h * 64 + 2 * (p % 32)
            else:
                h = 4 * (dt - 4) + p // 32
                chan = h * 64 + 2 * (p % 32) + 1
            perm[dt * 128 + p] = chan
    wqT = np.ascontiguousarray(np.asarray(L0_attn_wq_w)[perm, :].T)
    wkT = np.ascontiguousarray(np.asarray(L0_attn_wk_w)[perm, :].T)
    wvT = np.ascontiguousarray(np.asarray(L0_attn_wv_w).T)
    wfT = np.ascontiguousarray(np.asarray(L0_ffn_weight).T)
    bq = np.ascontiguousarray(np.asarray(L0_attn_wq_b)[perm].reshape(8, 128).T)
    bk = np.ascontiguousarray(np.asarray(L0_attn_wk_b)[perm].reshape(8, 128).T)
    bvrep = np.ascontiguousarray(np.broadcast_to(np.asarray(L0_attn_wv_b)[None, :], (128, D)))
    bfrep = np.ascontiguousarray(np.broadcast_to(np.asarray(L0_ffn_bias)[None, :], (128, D)))
    i = np.arange(HD // 2, dtype=np.float32)
    theta = (np.float32(10000.0) ** (np.float32(-2.0) * i / np.float32(HD))).astype(np.float32)
    m = np.arange(S, dtype=np.float32)
    ang = np.outer(m, theta).astype(np.float32)
    cos = np.cos(ang).astype(np.float32)
    sin = np.sin(ang).astype(np.float32)
    crep = np.ascontiguousarray(np.tile(cos.T, (4, 1)))
    srep = np.ascontiguousarray(np.tile(sin.T, (4, 1)))
    eye32 = np.eye(128, dtype=np.float32)
    eyebf = np.eye(128, dtype=ml_dtypes.bfloat16)
    consts = np.zeros((128, 8), dtype=np.float32)
    consts[:, C_025] = 0.25
    consts[:, C_ILN2] = np.float32(1.0 / LN2)
    consts[:, C_LN2] = np.float32(LN2)
    consts[:, C_EPS] = np.float32(1e-5)
    consts[:, C_SHIFT] = np.float32(SHIFT)
    return dict(wqT=wqT, wkT=wkT, wvT=wvT, wfT=wfT, bq=bq, bk=bk,
                bvrep=bvrep, bfrep=bfrep, crep=crep, srep=srep,
                eye32=eye32, eyebf=eyebf, consts=consts)


def kernel(L0_attn_x, L0_attn_wq_w, L0_attn_wq_b, L0_attn_wk_w, L0_attn_wk_b,
           L0_attn_wv_w, L0_attn_wv_b, L0_ffn_weight, L0_ffn_bias):
    shared = _host_prep(L0_attn_wq_w, L0_attn_wq_b, L0_attn_wk_w, L0_attn_wk_b,
                        L0_attn_wv_w, L0_attn_wv_b, L0_ffn_weight, L0_ffn_bias)
    x = np.asarray(L0_attn_x, dtype=np.float32)
    nc = build_nc()
    in_maps = []
    for b in range(B):
        m = dict(shared)
        m["x"] = np.ascontiguousarray(x[b])
        in_maps.append(m)
    res = run_bass_kernel_spmd(nc, in_maps, core_ids=list(range(B)))
    out = np.stack([np.asarray(r["out"]) for r in res.results], axis=0)
    return out.astype(np.float32)
